# version 03
# baseline (speedup 1.0000x reference)
import numpy as np
import ml_dtypes
from contextlib import ExitStack
import concourse.bass as bass
import concourse.mybir as mybir
from concourse.bass_utils import run_bass_kernel_spmd

F32 = mybir.dt.float32
BF16 = mybir.dt.bfloat16
AF = mybir.ActivationFunctionType
ALU = mybir.AluOpType
AX = mybir.AxisListType

T = 2048
D = 1024
NCH = 8
CW = T + 1
BFNP = ml_dtypes.bfloat16



class Reg:

    def __init__(self, t, n):
        self.t = t
        self.n = n
        self.off = 0
        self.stack = []

    def al(self, cols, rows=128):
        o = (self.off + 15) & ~15
        self.off = o + cols
        assert self.off <= self.n, f"region overflow {self.off} > {self.n}"
        return self.t[0:rows, o:o + cols]

    def mark(self):
        self.stack.append(self.off)

    def rel(self):
        self.off = self.stack.pop()

    def at(self, off, cols, rows=128):
        assert off + cols <= self.n
        return self.t[0:rows, off:off + cols]


class Bld:
    def __init__(self, nc, ctx):
        self.nc = nc
        self.ctx = ctx
        self.sems = {}
        self.cnt = {}
        self.uid = 0

    def sem(self, name):
        if name not in self.sems:
            self.sems[name] = self.ctx.enter_context(self.nc.semaphore(name))
            self.cnt[name] = 0
        return self.sems[name]

    def inc(self, ins, name, n=1):
        ins.then_inc(self.sem(name), n)
        self.cnt[name] += n
        return self.cnt[name]

    def val(self, name):
        self.sem(name)
        return self.cnt[name]

    def w(self, eng, *names):
        for n in names:
            v = self.val(n)
            if v:
                eng.wait_ge(self.sem(n), v)

    def sb(self, name, shape, dt):
        self.uid += 1
        return self.ctx.enter_context(
            self.nc.sbuf_tensor(f"{name}_{self.uid}", shape, dt))

    def ps(self, name, shape, dt=F32):
        self.uid += 1
        return self.ctx.enter_context(
            self.nc.psum_tensor(f"{name}_{self.uid}", shape, dt))


def make_common(b, block, nbf, nf32):
    cm = {}
    cm['pb'] = [b.ps("pb", [128, 512]) for _ in range(8)]
    cm['rbf'] = Reg(b.sb("RBF", [128, nbf], BF16), nbf)
    cm['rf'] = Reg(b.sb("RF32", [128, nf32], F32), nf32)
    cm['onesb'] = b.sb("onesb", [128, 1], BF16)
    cm['epsc'] = b.sb("epsc", [128, 1], F32)
    cm['onefc'] = b.sb("onefc", [128, 1], F32)
    cm['negbig'] = b.sb("negbig", [128, 1], F32)
    cm['e8c'] = b.sb("e8c", [128, 1], F32)
    cm['ones1f'] = b.sb("ones1f", [1, 128], F32)
    cm['ones1b'] = b.sb("ones1b", [1, 128], BF16)
    cm['hrow'] = b.sb("hrow", [1, 512], BF16)
    cm['lrow'] = b.sb("lrow", [1, 512], BF16)
    cm['l32row'] = b.sb("l32row", [1, 512], F32)
    cm['idf'] = b.sb("idf", [128, 128], F32)

    @block.vector
    def _(v):
        v.memset(cm['onesb'][:], 1.0)
        v.memset(cm['epsc'][:], 1e-5)
        v.memset(cm['onefc'][:], 1.0)
        v.memset(cm['negbig'][:], -1e30)
        v.memset(cm['e8c'][:], 1e-8)
        v.memset(cm['ones1f'][:], 1.0)
        b.inc(v.memset(cm['ones1b'][:], 1.0), 'V')

    @block.sync
    def _(s):
        b.inc(s.dma_start(out=cm['idf'][:], in_=b.idf_dram), 'L', 16)
    return cm




def emit_split_row(b, v, row_f32, n, cm):
    v.tensor_copy(cm['hrow'][:, 0:n], row_f32)
    v.tensor_sub(cm['l32row'][:, 0:n], row_f32, cm['hrow'][:, 0:n])
    return v.tensor_copy(cm['lrow'][:, 0:n], cm['l32row'][:, 0:n])


def emit_bcast_mm(b, t, psum_ap, n, cm, m=128, start=True, stop=True):
    t.matmul(psum_ap, cm['ones1b'][0:1, 0:m], cm['hrow'][:, 0:n],
             start=start, stop=False)
    return t.matmul(psum_ap, cm['ones1b'][0:1, 0:m], cm['lrow'][:, 0:n],
             start=False, stop=stop)




def emit_zero_cols(b, block, xs):
    @block.vector
    def _(v):
        for c in range(NCH - 1):
            v.memset(xs[:, c * CW:c * CW + 1], 0.0)
        b.inc(v.memset(xs[:, (NCH - 1) * CW:(NCH - 1) * CW + 1], 0.0), 'V')


def emit_stream_in_fm(b, block, x_fm_dram, xs):
    emit_zero_cols(b, block, xs)

    @block.sync
    def _(s):
        for c in range(NCH):
            b.inc(s.dma_start(out=xs[:, c * CW + 1:c * CW + 1 + T],
                              in_=x_fm_dram[c * 128:(c + 1) * 128, :]), 'L', 16)


def emit_stream_out_fm(b, block, xs, x_fm_dram, wait=('V', 'P', 'E')):
    @block.sync
    def _(s):
        b.w(s, *wait)
        for c in range(NCH):
            b.inc(s.dma_start(out=x_fm_dram[c * 128:(c + 1) * 128, :],
                              in_=xs[:, c * CW + 1:c * CW + 1 + T]), 'S', 16)
        s.wait_ge(b.sem('S'), b.val('S'))


def emit_stream_in_tok(b, block, x_tok_dram, xs, cm):
    stg = [cm['rf'].at(0, 1024), cm['rf'].at(1024, 1024)]
    ptA, ptB = cm['pb'][5], cm['pb'][6]
    emit_zero_cols(b, block, xs)
    for tt in range(16):
        sbuf = stg[tt % 2]

        @block.sync
        def _(s, tt=tt, sbuf=sbuf):
            if tt >= 1:
                s.wait_ge(b.sem('M'), b.val('M'))
            else:
                b.w(s, 'M', 'V')
            b.inc(s.dma_start(out=sbuf,
                              in_=x_tok_dram[tt * 128:(tt + 1) * 128, :]), 'Lx', 16)

        @block.tensor
        def _(t, tt=tt, sbuf=sbuf):
            t.wait_ge(b.sem('Lx'), b.val('Lx'))
            for c in range(NCH):
                pt = (ptA, ptB)[c % 2]
                if tt > 0 or c >= 2:
                    t.wait_ge(b.sem('V'), b.val('V') - 1 + c)
                ins = t.transpose(pt[:, 0:128], sbuf[:, c * 128:(c + 1) * 128],
                            cm['idf'][:])
                b.inc(ins, 'M')

        @block.vector
        def _(v, tt=tt):
            base = b.val('M') - NCH
            for c in range(NCH):
                pt = (ptA, ptB)[c % 2]
                v.wait_ge(b.sem('M'), base + c + 1)
                b.inc(v.tensor_copy(
                    xs[:, c * CW + 1 + tt * 128:c * CW + 1 + (tt + 1) * 128],
                    pt[:, 0:128]), 'V')


def emit_stream_out_tok(b, block, xs, x_tok_dram, cm, wait=('V', 'P', 'E')):
    stg = [cm['rf'].at(0, 1024), cm['rf'].at(1024, 1024)]
    ptA, ptB = cm['pb'][5], cm['pb'][6]
    for tt in range(16):
        sbuf = stg[tt % 2]

        @block.tensor
        def _(t, tt=tt):
            if tt == 0:
                b.w(t, *wait)
            for c in range(NCH):
                pt = (ptA, ptB)[c % 2]
                if tt > 0 or c >= 2:
                    t.wait_ge(b.sem('V'), b.val('V') - 1 + c)
                ins = t.transpose(pt[:, 0:128],
                            xs[:, c * CW + 1 + tt * 128:c * CW + 1 + (tt + 1) * 128],
                            cm['idf'][:])
                b.inc(ins, 'M')

        @block.vector
        def _(v, tt=tt, sbuf=sbuf):
            base = b.val('M') - NCH
            par = 'SA' if tt % 2 == 0 else 'SB'
            if tt >= 2:
                v.wait_ge(b.sem(par), b.val(par))
            for c in range(NCH):
                pt = (ptA, ptB)[c % 2]
                v.wait_ge(b.sem('M'), base + c + 1)
                b.inc(v.tensor_copy(sbuf[:, c * 128:(c + 1) * 128],
                                    pt[:, 0:128]), 'V')

        @block.sync
        def _(s, tt=tt, sbuf=sbuf):
            s.wait_ge(b.sem('V'), b.val('V'))
            b.inc(s.dma_start(out=x_tok_dram[tt * 128:(tt + 1) * 128, :],
                              in_=sbuf), 'SA' if tt % 2 == 0 else 'SB', 16)

    @block.sync
    def _(s):
        s.wait_ge(b.sem('SA'), b.val('SA'))
        s.wait_ge(b.sem('SB'), b.val('SB'))




def emit_ln(b, block, xs, xnb, cm, scr, shift=False, hs=None, wait=(),
            gain_col=None, bias_col=None, debug_stop=None, f32_inplace=False,
            out_cw=0):
    pb = cm['pb']
    xb2, sq2, nt = scr['xb2'], scr['sq2'], scr['nt']
    bcR, bcM = scr['bcR'], scr['bcM']
    rowA, rowB, rowC, rowD = scr['rows']

    def src(c, off0, ln_):
        off = 0 if (not shift or c < 4) else -1
        return xs[:, c * CW + 1 + off0 + off: c * CW + 1 + off0 + off + ln_]

    for c in range(NCH):
        sl = (c % 2) * T

        @block.gpsimd
        def _(g, c=c, sl=sl):
            if c == 0:
                b.w(g, *wait)
            if c >= 2:
                g.wait_ge(b.sem('M'), b.val('M'))
            b.inc(g.tensor_copy(xb2[:, sl:sl + T], src(c, 0, T)), 'P')

        @block.tensor
        def _(t, c=c, sl=sl):
            t.wait_ge(b.sem('P'), b.val('P'))
            for ic in range(4):
                ins = t.matmul(pb[ic][0:1, 0:512], cm['onesb'][:],
                         xb2[:, sl + ic * 512:sl + (ic + 1) * 512],
                         start=(c == 0), stop=(c == NCH - 1))
            b.inc(ins, 'M')

        @block.gpsimd
        def _(g, c=c, sl=sl):
            b.inc(g.tensor_tensor(sq2[:, sl:sl + T], xb2[:, sl:sl + T],
                                  xb2[:, sl:sl + T], ALU.mult), 'P')

        @block.tensor
        def _(t, c=c, sl=sl):
            t.wait_ge(b.sem('P'), b.val('P'))
            for ic in range(4):
                ins = t.matmul(pb[4 + ic][0:1, 0:512], cm['onesb'][:],
                         sq2[:, sl + ic * 512:sl + (ic + 1) * 512],
                         start=(c == 0), stop=(c == NCH - 1))
            b.inc(ins, 'M')

    if debug_stop == 'sums':
        return
    for ich in range(2):
        for ic2 in range(2):
            ic = ich * 2 + ic2

            @block.scalar
            def _(s, ic=ic):
                s.wait_ge(b.sem('M'), b.val('M'))
                s.wait_ge(b.sem('V'), b.val('V'))
                s.activation(rowA[:], pb[ic][0:1, 0:512], AF.Copy, scale=1.0 / D)
                b.inc(s.activation(rowB[:], pb[4 + ic][0:1, 0:512], AF.Copy,
                                   scale=1.0 / D), 'E')

            @block.vector
            def _(v, ic=ic):
                v.wait_ge(b.sem('E'), b.val('E'))
                v.tensor_tensor(rowC[:], rowA[:], rowA[:], ALU.mult)
                b.inc(v.tensor_sub(rowB[:], rowB[:], rowC[:]), 'V')

            @block.scalar
            def _(s, ic=ic):
                s.wait_ge(b.sem('V'), b.val('V'))
                s.activation(rowC[:], rowB[:], AF.Ln, bias=cm['epsc'][0:1, 0:1])
                b.inc(s.activation(rowD[:], rowC[:], AF.Exp, scale=-0.5), 'E')

            @block.vector
            def _(v, ic=ic):
                v.wait_ge(b.sem('E'), b.val('E'))
                v.tensor_tensor(rowC[:], rowA[:], rowD[:], ALU.mult)
                b.inc(v.tensor_scalar_mul(rowC[:], rowC[:], -1.0), 'V')

            for which, rsrc, bank, dst in ((0, rowD, ic, bcR), (1, rowC, 4 + ic, bcM)):
                @block.vector
                def _(v, rsrc=rsrc, which=which):
                    b.w(v, 'E')
                    if which == 1:
                        v.wait_ge(b.sem('M'), b.val('M'))
                    b.inc(emit_split_row(b, v, rsrc[:], 512, cm), 'V')

                @block.tensor
                def _(t, bank=bank):
                    b.w(t, 'V', 'E')
                    ins = emit_bcast_mm(b, t, pb[bank][:], 512, cm)
                    b.inc(ins, 'M')

                @block.scalar
                def _(s, bank=bank, dst=dst, ic2=ic2):
                    s.wait_ge(b.sem('M'), b.val('M'))
                    b.inc(s.copy(dst[:, ic2 * 512:(ic2 + 1) * 512], pb[bank][:]),
                          'E')

        if debug_stop == 'stats':
            continue
        for c in range(NCH):
            @block.gpsimd
            def _(g, c=c, ich=ich):
                g.wait_ge(b.sem('E'), b.val('E'))
                g.wait_ge(b.sem('V'), b.val('V'))
                b.inc(g.tensor_tensor(nt[:], src(c, ich * 1024, 1024), bcR[:],
                                      ALU.mult), 'P')

            @block.vector
            def _(v, c=c, ich=ich):
                v.wait_ge(b.sem('P'), b.val('P'))
                stw = T + out_cw
                o0 = (1 if out_cw else 0)
                dst = xnb[:, c * stw + o0 + ich * 1024:
                          c * stw + o0 + (ich + 1) * 1024]
                if gain_col is None:
                    b.inc(v.tensor_add(dst, nt[:], bcM[:]), 'V')
                else:
                    v.tensor_add(nt[:], nt[:], bcM[:])
                    b.inc(v.tensor_scalar(dst, nt[:], gain_col[:, c:c + 1],
                                          bias_col[:, c:c + 1],
                                          ALU.mult, ALU.add), 'V')
                if hs is not None:
                    xd = xs[:, c * CW + 1 + ich * 1024:
                            c * CW + 1 + (ich + 1) * 1024]
                    v.scalar_tensor_tensor(xd, nt[:], hs, xd, ALU.mult, ALU.add)
                    b.inc(v.scalar_tensor_tensor(xd, bcM[:], hs, xd,
                                                 ALU.mult, ALU.add), 'V')
                if f32_inplace:
                    xd = xs[:, c * CW + 1 + ich * 1024:
                            c * CW + 1 + (ich + 1) * 1024]
                    b.inc(v.tensor_add(xd, nt[:], bcM[:]), 'V')




def emit_cols(b, block, src_row_ap, cols_sb, n, cm, scale=1.0, nlast=128):
    row = cm['rf'].at(0, 4096, 1)
    pt = cm['pb'][6]
    for p0 in range(0, n, 32):
        pn = min(32, n - p0)
        wlast = pn * 128 if (p0 + pn < n or nlast == 128) \
            else (pn - 1) * 128 + nlast

        @block.sync
        def _(s, p0=p0, pn=pn, wlast=wlast):
            b.w(s, 'V')
            b.inc(s.dma_start(out=row[:, 0:wlast],
                              in_=src_row_ap[:, p0 * 128:p0 * 128 + wlast]),
                  'Lc', 16)

        for c in range(pn):
            rows_c = 128 if (p0 + c < n - 1 or nlast == 128) else nlast

            @block.tensor
            def _(t, c=c, rows_c=rows_c):
                t.wait_ge(b.sem('Lc'), b.val('Lc'))
                b.w(t, 'L')
                t.wait_ge(b.sem('V'), b.val('V'))
                ins = t.transpose(pt[0:rows_c, 0:1],
                                  row[:, c * 128:c * 128 + rows_c],
                                  cm['idf'][0:1, 0:1])
                b.inc(ins, 'M')

            @block.vector
            def _(v, c=c, p0=p0, rows_c=rows_c):
                v.wait_ge(b.sem('M'), b.val('M'))
                b.inc(v.tensor_scalar_mul(cols_sb[0:rows_c, p0 + c:p0 + c + 1],
                                          pt[0:rows_c, 0:1], scale), 'V')


def emit_wload(b, block, w_dram, wsb, din, dout, wait=()):
    @block.gpsimd
    def _(g):
        b.w(g, *wait)
        for dc in range(din // 128):
            b.inc(g.dma_start(out=wsb[:, dc * dout:(dc + 1) * dout],
                              in_=w_dram[dc * 128:(dc + 1) * 128, :]), 'G', 16)


def emit_row_load(b, block, vec_ap, row_sb):
    @block.sync
    def _(s):
        b.inc(s.dma_start(out=row_sb[:], in_=vec_ap), 'L', 16)




def build_ffn1(first=False, last=False, upto='moe', debug=False):
    nc = bass.Bass("TRN2", target_bir_lowering=False)
    ctx = ExitStack()
    b = Bld(nc, ctx)

    dr = {}
    dr['xi'] = nc.dram_tensor("xi", [T, D] if first else [D, T], F32,
                              kind="ExternalInput").ap()
    dr['xo'] = nc.dram_tensor("xo", [T, D] if last else [D, T], F32,
                              kind="ExternalOutput").ap()
    for nm, shp in [('q_w', [D, D]), ('k_w', [D, 512]), ('v_w', [D, 512]),
                    ('o_w', [D, D]), ('router_w', [D, 8]),
                    ('w1', [8, D, 2048]), ('w2', [8, 2048, D])]:
        dr[nm] = nc.dram_tensor(nm, shp, F32, kind="ExternalInput").ap()
    for nm, shp in [('q_b', [D]), ('k_b', [512]), ('v_b', [512]), ('o_b', [D]),
                    ('router_b', [8]), ('b1', [8, 2048]), ('b2', [8, D])]:
        dr[nm] = nc.dram_tensor(nm, shp, F32, kind="ExternalInput").ap()
    b.idf_dram = nc.dram_tensor("idf", [128, 128], F32, kind="ExternalInput").ap()
    if debug:
        dbg = {}
        for nm, shp in [('d_esb', [128, 8]), ('d_tsb', [128, 8]),
                        ('d_rsc', [128, 8]), ('d_selb', [128, 8]),
                        ('d_logit', [128, 8]),
                        ('d_xnb', [128, NCH * T]), ('d_qop', [128, 4 * T]),
                        ('d_kop', [128, 4 * T]), ('d_vop', [128, 16 * 16 * 33]),
                        ('d_oall', [128, 4 * T]), ('d_mlab', [128, NCH * T]),
                        ('d_ctok', [128, 128]), ('d_oun', [33, T])]:
            dbg[nm] = nc.dram_tensor(nm, shp, F32, kind="ExternalOutput").ap()

    QSCALE = 1.0 / np.sqrt(32.0)

    with ctx:
        block = ctx.enter_context(nc.Block())
        cm = make_common(b, block, nbf=54544, nf32=5696)
        pb = cm['pb']
        rf, rbf = cm['rf'], cm['rbf']
        xs = b.sb("xs", [128, NCH * CW], F32)

        ctok = b.sb("ctok", [128, 128], F32)
        b1c = b.sb("b1c", [128, 128], F32)
        b2sb = b.sb("b2sb", [8, D], F32)
        qbc = b.sb("qbc", [128, 8], F32)
        kbc = b.sb("kbc", [128, 4], F32)
        obc = b.sb("obc", [128, 8], F32)
        vbrow = b.sb("vbrow", [1, 512], F32)
        rbrow = b.sb("rbrow", [1, 8], F32)
        rsc = b.sb("rsc", [128, 8], F32)
        esb = b.sb("esb", [128, 8], F32)
        tsb = b.sb("tsb", [128, 8], F32)
        selb = b.sb("selb", [128, 8], F32)

        if first:
            emit_stream_in_tok(b, block, dr['xi'], xs, cm)
        else:
            emit_stream_in_fm(b, block, dr['xi'], xs)

        emit_cols(b, block, dr['q_b'].rearrange("(o f) -> o f", o=1), qbc, 8,
                  cm, scale=QSCALE)
        emit_cols(b, block, dr['k_b'].rearrange("(o f) -> o f", o=1), kbc, 4, cm)
        emit_cols(b, block, dr['o_b'].rearrange("(o f) -> o f", o=1), obc, 8, cm)
        emit_cols(b, block, dr['b1'].rearrange("e f -> () (e f)"), b1c, 128, cm)
        emit_row_load(b, block, dr['v_b'].rearrange("(o f) -> o f", o=1), vbrow)
        emit_row_load(b, block, dr['router_b'].rearrange("(o f) -> o f", o=1),
                      rbrow)
        vbh = rbf.at(53504, 512, 1)
        vbl = rbf.at(54016, 512, 1)
        rbh = b.sb("rbh", [1, 8], BF16)
        rbl = b.sb("rbl", [1, 8], BF16)

        @block.sync
        def _(s):
            b.inc(s.dma_start(out=b2sb[:], in_=dr['b2']), 'L', 16)

        @block.vector
        def _(v):
            b.w(v, 'L')
            v.tensor_copy(vbh[:], vbrow[:])
            v.tensor_sub(cm['l32row'][:], vbrow[:], vbh[:])
            v.tensor_copy(vbl[:], cm['l32row'][:])
            v.tensor_copy(rbh[:], rbrow[:])
            v.tensor_sub(cm['l32row'][:, 0:8], rbrow[:], rbh[:])
            b.inc(v.tensor_copy(rbl[:], cm['l32row'][:, 0:8]), 'V')

        b2h = rbf.at(45056, D, 8)
        b2l = rbf.at(46080, D, 8)
        cfh = rbf.at(47104, 512, 8)
        cfl = rbf.at(47616, 512, 8)

        XNB, XB2, SQ2 = 0, 16384, 20480
        QOP, KOP, VOP, OALL, WBUF, PTB = 16384, 24576, 32768, 41216, 49408, 53504
        MLAB, OW = 0, 49408
        H1S, W2E, W1BLK = 16384, 24576, 40960
        xnb = rbf.at(XNB, NCH * T)
        lnscr = {
            'xb2': rbf.at(XB2, 2 * T), 'sq2': rbf.at(SQ2, 2 * T),
            'nt': rf.at(0, 1024), 'bcR': rf.at(1024, 1024),
            'bcM': rf.at(2048, 1024),
            'rows': [rf.at(3072 + i * 512, 512, 1) for i in range(4)],
        }

        stop_after = {'io': -1, 'ln': 0, 'proj': 1, 'attn': 2, 'mla': 3,
                      'router': 4, 'moe': 5}.get(upto, 0)

        def finish():
            if last:
                emit_stream_out_tok(b, block, xs, dr['xo'], cm)
            else:
                emit_stream_out_fm(b, block, xs, dr['xo'])

        if stop_after == -1:
            if last:
                emit_stream_out_tok(b, block, xs, dr['xo'], cm,
                                    wait=('V', 'L'))
            else:
                emit_stream_out_fm(b, block, xs, dr['xo'], wait=('V', 'L'))
            return nc

        emit_ln(b, block, xs, xnb, cm, lnscr, hs=0.5, wait=('L', 'V', 'M'),
                debug_stop=(upto[3:] if upto.startswith('ln:') else None))
        if upto.startswith('ln:'):
            finish()
            return nc

        if stop_after == 0:
            finish()
            return nc

        if debug:
            @block.gpsimd
            def _(s):
                b.w(s, 'E', 'V', 'M', 'P')
                b.inc(s.dma_start(out=dbg['d_xnb'], in_=xnb), 'S', 16)

        qop = rbf.at(QOP, 4 * T)
        kop = rbf.at(KOP, 4 * T)
        vop = rbf.at(VOP, 16 * 16 * 33)
        oall = rbf.at(OALL, 4 * T)
        wbuf = rbf.at(WBUF, 4 * D)

        def qslice(h, i0, ln_):
            return qop[32 * (h % 4):32 * (h % 4) + 32,
                       (h // 4) * T + i0:(h // 4) * T + i0 + ln_]

        def kslice(h, j0, ln_):
            return kop[32 * (h % 4):32 * (h % 4) + 32,
                       (h // 4) * T + j0:(h // 4) * T + j0 + ln_]

        @block.vector
        def _(v):
            b.w(v, 'P')
            b.inc(v.memset(vop, 1.0), 'V')

        qwsel = rbf.at(OALL, 4096)
        for dch in range(2):
            @block.gpsimd
            def _(gq, dch=dch):
                if dch == 1:
                    gq.wait_ge(b.sem('V'), b.val('V'))
                for dc in range(4):
                    b.inc(gq.dma_start(
                        out=wbuf[:, dc * D:(dc + 1) * D],
                        in_=dr['q_w'][(dch * 4 + dc) * 128:
                                      (dch * 4 + dc + 1) * 128, :]), 'G', 16)

            @block.vector
            def _(v, dch=dch):
                v.wait_ge(b.sem('G'), b.val('G'))
                for dc in range(4):
                    for g in range(4):
                        sel = wbuf[:, dc * D + g * 256: dc * D + (g + 1) * 256] \
                            .rearrange("p (h k) -> p h k", h=4)[:, :, 0:32]
                        ins = v.tensor_copy(
                            qwsel[:, (dch * 4 + dc) * 512 + g * 128:
                                  (dch * 4 + dc) * 512 + (g + 1) * 128], sel)
                b.inc(ins, 'V')

        for g in range(4):
            for ic in range(4):
                @block.tensor
                def _(t, g=g, ic=ic):
                    b.w(t, 'V', 'E')
                    for dc in range(NCH):
                        ins = t.matmul(pb[ic % 2][:],
                                 qwsel[:, dc * 512 + g * 128: dc * 512 + (g + 1) * 128],
                                 xnb[:, dc * T + ic * 512: dc * T + (ic + 1) * 512],
                                 start=(dc == 0), stop=(dc == NCH - 1))
                    b.inc(ins, 'M')

                @block.scalar
                def _(s, g=g, ic=ic):
                    s.wait_ge(b.sem('M'), b.val('M'))
                    for m in range(4):
                        h = 4 * g + m
                        ins = s.activation(qslice(h, ic * 512, 512),
                                           pb[ic % 2][32 * m:32 * m + 32, :],
                                           AF.Identity,
                                           bias=qbc[(h % 2) * 64:(h % 2) * 64 + 32,
                                                    h // 2:h // 2 + 1],
                                           scale=QSCALE)
                    b.inc(ins, 'E')

        emit_wload(b, block, dr['k_w'], wbuf[:, 0:NCH * 512], D, 512,
                   wait=('M',))
        for g in range(4):
            for ic in range(4):
                @block.tensor
                def _(t, g=g, ic=ic):
                    b.w(t, 'G', 'V', 'E')
                    for dc in range(NCH):
                        ins = t.matmul(pb[2 + ic % 2][:],
                                 wbuf[:, dc * 512 + g * 128: dc * 512 + (g + 1) * 128],
                                 xnb[:, dc * T + ic * 512: dc * T + (ic + 1) * 512],
                                 start=(dc == 0), stop=(dc == NCH - 1))
                    b.inc(ins, 'M')

                @block.scalar
                def _(s, g=g, ic=ic):
                    s.wait_ge(b.sem('M'), b.val('M'))
                    for m in range(4):
                        h = 4 * g + m
                        ins = s.activation(kslice(h, ic * 512, 512),
                                           pb[2 + ic % 2][32 * m:32 * m + 32, :],
                                           AF.Identity,
                                           bias=kbc[32 * m:32 * m + 32, g:g + 1])
                    b.inc(ins, 'E')

        emit_wload(b, block, dr['v_w'], wbuf[:, 0:NCH * 512], D, 512,
                   wait=('M', 'E'))
        for tt in range(16):
            @block.tensor
            def _(t, tt=tt):
                b.w(t, 'G', 'V', 'E')
                for dc in range(NCH):
                    t.matmul(pb[4 + tt % 2][:],
                             xnb[:, dc * T + tt * 128: dc * T + (tt + 1) * 128],
                             wbuf[:, dc * 512:(dc + 1) * 512],
                             start=(dc == 0), stop=False)
                t.matmul(pb[4 + tt % 2][:], cm['ones1b'][:], vbh[:],
                         start=False, stop=False)
                ins = t.matmul(pb[4 + tt % 2][:], cm['ones1b'][:], vbl[:],
                         start=False, stop=True)
                b.inc(ins, 'M')

            @block.scalar
            def _(s, tt=tt):
                s.wait_ge(b.sem('M'), b.val('M'))
                for h in range(16):
                    ins = s.activation(
                        vop[:, (h * 16 + tt) * 33:(h * 16 + tt) * 33 + 32],
                        pb[4 + tt % 2][:, h * 32:(h + 1) * 32], AF.Copy)
                b.inc(ins, 'E')

        if stop_after == 1:
            finish()
            return nc


        if debug:
            @block.gpsimd
            def _(s):
                b.w(s, 'E', 'V', 'M', 'P')
                b.inc(s.dma_start(out=dbg['d_qop'], in_=qop), 'S', 16)

        if debug:
            @block.gpsimd
            def _(s):
                b.w(s, 'E', 'V', 'M', 'P')
                b.inc(s.dma_start(out=dbg['d_kop'], in_=kop), 'S', 16)

        if debug:
            @block.gpsimd
            def _(s):
                b.w(s, 'E', 'V', 'M', 'P')
                b.inc(s.dma_start(out=dbg['d_vop'], in_=vop), 'S', 16)

        ptb = rbf.at(PTB, 2 * 512)
        oun = rf.at(0, 512, 33)
        rrow = rf.at(512, 512, 1)
        for h in range(16):
            for ic in range(4):
                def sc_mm(t, jt, h=h, ic=ic):
                    t.wait_ge(b.sem('E'), b.val('E'))
                    ins = t.matmul(pb[jt % 2][:], kslice(h, jt * 128, 128),
                             qslice(h, ic * 512, 512), start=True, stop=True,
                             tile_position=(32 * (h % 4), 0))
                    b.inc(ins, 'M')

                def ex_op(s, jt):
                    s.wait_ge(b.sem('M'), b.val('M'))
                    b.inc(s.activation(ptb[:, (jt % 2) * 512:(jt % 2) * 512 + 512],
                                       pb[jt % 2][:], AF.Exp), 'E')

                def pv_mm(t, jt, h=h):
                    t.wait_ge(b.sem('E'), b.val('E'))
                    ins = t.matmul(pb[2][0:33, :],
                             vop[:, (h * 16 + jt) * 33:(h * 16 + jt + 1) * 33],
                             ptb[:, (jt % 2) * 512:(jt % 2) * 512 + 512],
                             start=(jt == 0), stop=(jt == 15))
                    b.inc(ins, 'M')

                for jt in range(16):
                    if jt >= 1:
                        @block.scalar
                        def _(s, jt=jt):
                            ex_op(s, jt - 1)

                    @block.tensor
                    def _(t, jt=jt, h=h, ic=ic):
                        if h == 0 and ic == 0 and jt == 0:
                            b.w(t, 'E', 'V')
                        sc_mm(t, jt)
                        if jt >= 1:
                            pv_mm(t, jt - 1)

                @block.scalar
                def _(s):
                    ex_op(s, 15)

                @block.tensor
                def _(t):
                    pv_mm(t, 15)

                @block.scalar
                def _(s, h=h, ic=ic):
                    s.wait_ge(b.sem('M'), b.val('M'))
                    s.wait_ge(b.sem('V'), b.val('V'))
                    s.copy(oun[:], pb[2][0:33, :])
                    s.activation(rrow[:], oun[32:33, :], AF.Ln)
                    b.inc(s.activation(rrow[:], rrow[:], AF.Exp, scale=-1.0),
                          'E')

                @block.vector
                def _(v, h=h, ic=ic):
                    b.w(v, 'E')
                    v.wait_ge(b.sem('M'), b.val('M'))
                    b.inc(emit_split_row(b, v, rrow[:], 512, cm), 'V')

                @block.tensor
                def _(t, h=h, ic=ic):
                    b.w(t, 'E', 'V')
                    ins = emit_bcast_mm(b, t, pb[3][0:32, :], 512, cm, m=32)
                    b.inc(ins, 'M')

                @block.vector
                def _(v, h=h, ic=ic):
                    v.wait_ge(b.sem('M'), b.val('M'))
                    b.inc(v.tensor_tensor(
                        oall[32 * (h % 4):32 * (h % 4) + 32,
                             (h // 4) * T + ic * 512:(h // 4) * T + (ic + 1) * 512],
                        oun[0:32, :], pb[3][0:32, :], ALU.mult), 'V')

        if stop_after == 2:
            finish()
            return nc


        if debug:
            @block.gpsimd
            def _(s):
                b.w(s, 'E', 'V', 'M', 'P')
                b.inc(s.dma_start(out=dbg['d_oall'], in_=oall), 'S', 16)

        mlab = rbf.at(MLAB, NCH * T)
        ow = rbf.at(OW, 4 * D)
        emit_wload(b, block, dr['o_w'][0:512, :], ow, 512, D, wait=('M', 'E'))
        for oc in range(NCH):
            for ic in range(4):
                @block.tensor
                def _(t, oc=oc, ic=ic):
                    b.w(t, 'G', 'V', 'E')
                    for g in range(4):
                        ins = t.matmul(pb[4 + (oc * 4 + ic) % 2][:],
                                 ow[:, g * D + oc * 128: g * D + (oc + 1) * 128],
                                 oall[:, g * T + ic * 512: g * T + (ic + 1) * 512],
                                 start=(g == 0), stop=(g == 3))
                    b.inc(ins, 'M')

                @block.scalar
                def _(s, oc=oc, ic=ic):
                    s.wait_ge(b.sem('M'), b.val('M'))
                    b.inc(s.activation(
                        mlab[:, oc * T + ic * 512: oc * T + (ic + 1) * 512],
                        pb[4 + (oc * 4 + ic) % 2][:], AF.Identity,
                        bias=obc[:, oc:oc + 1]), 'E')

        if stop_after == 3:
            finish()
            return nc

        rweight = b.sb("rw", [128, NCH * 8], BF16)
        emit_wload(b, block, dr['router_w'], rweight, D, 8)
        for tt in range(16):
            @block.tensor
            def _(t, tt=tt):
                b.w(t, 'G', 'E')
                if debug and tt == 1:
                    b.w(t, 'S')
                for dc in range(NCH):
                    t.matmul(pb[5][0:128, 0:8],
                             mlab[:, dc * T + tt * 128: dc * T + (tt + 1) * 128],
                             rweight[:, dc * 8:(dc + 1) * 8],
                             start=(dc == 0), stop=False)
                t.matmul(pb[5][0:128, 0:8], cm['ones1b'][:], rbh[:],
                         start=False, stop=False)
                ins = t.matmul(pb[5][0:128, 0:8], cm['ones1b'][:], rbl[:],
                         start=False, stop=True)
                b.inc(ins, 'M')

            @block.vector
            def _(v, tt=tt):
                v.wait_ge(b.sem('M'), b.val('M'))
                b.inc(v.reduce_max(rsc[:, 0:1], pb[5][0:128, 0:8], axis=AX.X,
                                   negate=True), 'V')

            @block.scalar
            def _(s, tt=tt):
                s.wait_ge(b.sem('V'), b.val('V'))
                b.inc(s.activation(esb[:], pb[5][0:128, 0:8], AF.Exp,
                                   bias=rsc[:, 0:1], accum_out=rsc[:, 1:2]), 'E')

            if debug and tt == 0:
                @block.sync
                def _(s):
                    b.w(s, 'E')
                    b.inc(s.dma_start(out=dbg['d_logit'],
                                      in_=pb[5][0:128, 0:8]
                                      if False else esb[:]), 'S', 16)

            @block.vector
            def _(v, tt=tt):
                v.wait_ge(b.sem('E'), b.val('E'))

                def step(ins):
                    b.inc(ins, 'V')
                    v.wait_ge(b.sem('V'), b.val('V'))
                step(v.reduce_max(rsc[:, 5:6], esb[:], axis=AX.X))
                step(v.tensor_scalar(tsb[:], esb[:], rsc[:, 5:6], None,
                                     ALU.is_ge))
                step(v.scalar_tensor_tensor(tsb[:], tsb[:], cm['negbig'][:],
                                            esb[:], ALU.mult, ALU.add))
                step(v.reduce_max(rsc[:, 2:3], tsb[:], axis=AX.X))
                step(v.scalar_tensor_tensor(rsc[:, 3:4], rsc[:, 1:2],
                                            cm['e8c'][:], rsc[:, 2:3],
                                            ALU.mult, ALU.add))
                step(v.tensor_add(rsc[:, 3:4], rsc[:, 3:4], rsc[:, 5:6]))
                step(v.reciprocal(rsc[:, 4:5], rsc[:, 3:4]))
                step(v.tensor_scalar(selb[:], esb[:], rsc[:, 2:3], None,
                                     ALU.is_ge))
                b.inc(v.scalar_tensor_tensor(ctok[:, tt * 8:(tt + 1) * 8],
                                             esb[:], rsc[:, 4:5], selb[:],
                                             ALU.mult, ALU.mult), 'V')

            if debug and tt == 0:
                @block.sync
                def _(s):
                    b.w(s, 'V')
                    b.inc(s.dma_start(out=dbg['d_esb'], in_=esb[:]), 'S', 16)
                    b.inc(s.dma_start(out=dbg['d_tsb'], in_=tsb[:]), 'S', 16)
                    b.inc(s.dma_start(out=dbg['d_rsc'], in_=rsc[:]), 'S', 16)
                    b.inc(s.dma_start(out=dbg['d_selb'], in_=selb[:]), 'S', 16)

        if stop_after == 4:
            finish()
            return nc


        if debug:
            @block.gpsimd
            def _(s):
                b.w(s, 'E', 'V', 'M', 'P')
                b.inc(s.dma_start(out=dbg['d_mlab'], in_=mlab), 'S', 16)

        if debug:
            @block.sync
            def _(s):
                b.w(s, 'E', 'V', 'M', 'P')
                b.inc(s.dma_start(out=dbg['d_ctok'], in_=ctok[:]), 'S', 16)

        @block.vector
        def _(v):
            b.w(v, 'L', 'E')
            v.tensor_copy(b2h[:], b2sb[:])
            v.tensor_sub(b2sb[:], b2sb[:], b2h[:])
            b.inc(v.tensor_copy(b2l[:], b2sb[:]), 'V')

        mo = rf.at(0, NCH * 512)
        cbce = rf.at(4096, 512)
        cfmq = rf.at(4608, 512, 8)
        crowq = rf.at(5120, 512, 1)
        h1s = rbf.at(H1S, 16 * 512)
        w2e = rbf.at(W2E, 16 * D)
        w1blk = rbf.at(W1BLK, NCH * 512)

        for icq in range(4):
            tq = icq * 512
            for q4 in range(4):
                tt = icq * 4 + q4

                @block.tensor
                def _(t, tt=tt, q4=q4):
                    b.w(t, 'V', 'E')
                    ins = t.transpose(pb[6][0:8, 0:128], ctok[:, tt * 8:(tt + 1) * 8],
                                cm['idf'][:])
                    b.inc(ins, 'M')

                @block.vector
                def _(v, q4=q4):
                    v.wait_ge(b.sem('M'), b.val('M'))
                    b.inc(v.tensor_copy(cfmq[:, q4 * 128:(q4 + 1) * 128],
                                        pb[6][0:8, 0:128]), 'V')

            @block.vector
            def _(v):
                v.tensor_copy(cfh[:], cfmq)
                v.tensor_sub(cfmq, cfmq, cfh[:])
                b.inc(v.tensor_copy(cfl[:], cfmq), 'V')

            for e in range(8):
                @block.gpsimd
                def _(g, icq=icq, e=e):
                    g.wait_ge(b.sem('M'), b.val('M'))
                    for fq in range(4):
                        b.inc(g.dma_start(
                            out=w2e[:, fq * 4 * D:(fq + 1) * 4 * D]
                            .rearrange("p (fc o) -> p fc o", fc=4),
                            in_=dr['w2'][e, fq * 512:(fq + 1) * 512, :]
                            .rearrange("(fc p) o -> p fc o", p=128)), 'G', 16)

                for q4 in range(4):
                    tt = icq * 4 + q4

                    @block.tensor
                    def _(t, tt=tt, q4=q4, e=e):
                        b.w(t, 'V', 'E')
                        ins = t.transpose(pb[6][0:1, 128:256],
                                    ctok[:, tt * 8 + e:tt * 8 + e + 1],
                                    cm['idf'][:])
                        b.inc(ins, 'M')

                    @block.vector
                    def _(v, q4=q4):
                        v.wait_ge(b.sem('M'), b.val('M'))
                        b.inc(v.tensor_copy(crowq[:, q4 * 128:(q4 + 1) * 128],
                                            pb[6][0:1, 128:256]), 'V')

                @block.vector
                def _(v, e=e):
                    b.w(v, 'E')
                    v.wait_ge(b.sem('M'), b.val('M'))
                    b.inc(emit_split_row(b, v, crowq, 512, cm), 'V')

                @block.tensor
                def _(t, e=e):
                    b.w(t, 'V', 'E')
                    ins = emit_bcast_mm(b, t, pb[4][:], 512, cm)
                    b.inc(ins, 'M')

                @block.scalar
                def _(s, e=e):
                    s.wait_ge(b.sem('M'), b.val('M'))
                    s.wait_ge(b.sem('V'), b.val('V'))
                    b.inc(s.copy(cbce, pb[4][:]), 'E')

                for fb in range(4):
                    @block.gpsimd
                    def _(g, icq=icq, e=e, fb=fb):
                        g.wait_ge(b.sem('M'), b.val('M'))
                        b.inc(g.dma_start(
                            out=w1blk.rearrange("p (dc f) -> p dc f", dc=8),
                            in_=dr['w1'][e][:, fb * 512:(fb + 1) * 512]
                            .rearrange("(dc p) f -> p dc f", p=128)), 'G', 16)

                    for fsub in range(4):
                        fc = fb * 4 + fsub

                        @block.tensor
                        def _(t, icq=icq, e=e, fb=fb, fsub=fsub, fc=fc, tq=tq):
                            t.wait_ge(b.sem('G'), b.val('G'))
                            t.wait_ge(b.sem('V'), b.val('V'))
                            t.wait_ge(b.sem('E'), b.val('E'))
                            for dc in range(NCH):
                                ins = t.matmul(pb[fc % 2][:],
                                         w1blk[:, dc * 512 + fsub * 128:
                                               dc * 512 + (fsub + 1) * 128],
                                         mlab[:, dc * T + tq: dc * T + tq + 512],
                                         start=(dc == 0), stop=(dc == NCH - 1))
                            b.inc(ins, 'M')

                        @block.scalar
                        def _(s, e=e, fc=fc):
                            s.wait_ge(b.sem('M'), b.val('M'))
                            b.inc(s.activation(h1s[:, fc * 512:(fc + 1) * 512],
                                               pb[fc % 2][:], AF.Gelu,
                                               bias=b1c[:, e * 16 + fc:
                                                        e * 16 + fc + 1]), 'E')

                        @block.vector
                        def _(v, fc=fc):
                            v.wait_ge(b.sem('E'), b.val('E'))
                            b.inc(v.tensor_tensor(h1s[:, fc * 512:(fc + 1) * 512],
                                                  h1s[:, fc * 512:(fc + 1) * 512],
                                                  cbce, ALU.mult), 'V')

                for oc in range(NCH):
                    @block.tensor
                    def _(t, icq=icq, e=e, oc=oc):
                        t.wait_ge(b.sem('V'), b.val('V'))
                        for fc in range(16):
                            ins = t.matmul(pb[2 + oc % 2][:],
                                           w2e[:, fc * D + oc * 128:
                                               fc * D + (oc + 1) * 128],
                                           h1s[:, fc * 512:(fc + 1) * 512],
                                           start=(fc == 0),
                                           stop=(fc == 15 and e != 7))
                        if e == 7:
                            t.matmul(pb[2 + oc % 2][:],
                                     b2h[0:8, oc * 128:(oc + 1) * 128],
                                     cfh[:], start=False, stop=False)
                            t.matmul(pb[2 + oc % 2][:],
                                     b2h[0:8, oc * 128:(oc + 1) * 128],
                                     cfl[:], start=False, stop=False)
                            ins = t.matmul(pb[2 + oc % 2][:],
                                     b2l[0:8, oc * 128:(oc + 1) * 128],
                                     cfh[:], start=False, stop=True)
                        b.inc(ins, 'M')

                    @block.vector
                    def _(v, e=e, oc=oc):
                        v.wait_ge(b.sem('M'), b.val('M'))
                        dst = mo[:, oc * 512:(oc + 1) * 512]
                        if e == 0:
                            b.inc(v.tensor_copy(dst, pb[2 + oc % 2][:]), 'V')
                        else:
                            b.inc(v.tensor_add(dst, dst, pb[2 + oc % 2][:]), 'V')

            for oc in range(NCH):
                @block.vector
                def _(v, icq=icq, oc=oc, tq=tq):
                    b.inc(v.scalar_tensor_tensor(
                        xs[:, oc * CW + 1 + tq: oc * CW + 1 + tq + 512],
                        mo[:, oc * 512:(oc + 1) * 512], 0.15,
                        xs[:, oc * CW + 1 + tq: oc * CW + 1 + tq + 512],
                        ALU.mult, ALU.add), 'V')

        finish()

    return nc




def build_attn1(first=False, last=False, upto='all'):
    nc = bass.Bass("TRN2", target_bir_lowering=False)
    ctx = ExitStack()
    b = Bld(nc, ctx)
    NK = 4096

    dr = {}
    dr['xi'] = nc.dram_tensor("xi", [T, D] if first else [D, T], F32,
                              kind="ExternalInput").ap()
    dr['xo'] = nc.dram_tensor("xo", [T, D] if last else [D, T], F32,
                              kind="ExternalOutput").ap()
    for nm, shp in [('keys', [NK, D]), ('values', [NK, D]),
                    ('Wq_w', [D, D]), ('Wk_w', [D, D]), ('Wv_w', [D, D])]:
        dr[nm] = nc.dram_tensor(nm, shp, F32, kind="ExternalInput").ap()
    for nm in ['Wq_b', 'Wk_b', 'Wv_b']:
        dr[nm] = nc.dram_tensor(nm, [D], F32, kind="ExternalInput").ap()
    b.idf_dram = nc.dram_tensor("idf", [128, 128], F32, kind="ExternalInput").ap()
    kscr = nc.dram_tensor("kscr", [D, NK], BF16, kind="Internal").ap()
    vscr = nc.dram_tensor("vscr", [NK, D], BF16, kind="Internal").ap()

    with ctx:
        block = ctx.enter_context(nc.Block())
        cm = make_common(b, block, nbf=57360, nf32=5696)
        pb = cm['pb']
        rf, rbf = cm['rf'], cm['rbf']
        xs = b.sb("xs", [128, NCH * CW], F32)
        qbc = b.sb("qbc", [128, 8], F32)
        kbc = b.sb("kbc", [128, 8], F32)

        XNB, QOP, WT, FM = 0, 16384, 32768, 40960
        KH = 40960
        xnb = rbf.at(XNB, NCH * T)
        qop = rbf.at(QOP, NCH * T)
        wt = rbf.at(WT, NCH * D)
        fmb = rbf.at(FM, NCH * T)
        kst = rbf.at(0, 1024)
        vbh = rbf.at(2048, D, 1)
        vbl = rbf.at(3072, D, 1)
        vbrow = rf.at(4608, 1024, 1)
        lnscr = {
            'xb2': rbf.at(16384, 2 * T), 'sq2': rbf.at(20480, 2 * T),
            'nt': rf.at(0, 1024), 'bcR': rf.at(1024, 1024),
            'bcM': rf.at(2048, 1024),
            'rows': [rf.at(3072 + i * 512, 512, 1) for i in range(4)],
        }

        if first:
            emit_stream_in_tok(b, block, dr['xi'], xs, cm)
        else:
            emit_stream_in_fm(b, block, dr['xi'], xs)

        emit_cols(b, block, dr['Wq_b'].rearrange("(o f) -> o f", o=1), qbc, 8,
                  cm, scale=0.125)
        emit_cols(b, block, dr['Wk_b'].rearrange("(o f) -> o f", o=1), kbc, 8, cm)


        emit_ln(b, block, xs, xnb, cm, lnscr, hs=1.0, wait=('L', 'V', 'M'))

        def fin_():
            if last:
                emit_stream_out_tok(b, block, xs, dr['xo'], cm,
                                    wait=('V', 'P', 'E', 'S'))
            else:
                emit_stream_out_fm(b, block, xs, dr['xo'],
                                   wait=('V', 'P', 'E', 'S'))

        if upto == 'ln':
            fin_()
            return nc

        def emit_wT(w_dram):
            stg = rf.at(0, 1024)
            for oc in range(NCH):
                @block.sync
                def _(s, oc=oc):
                    s.wait_ge(b.sem('M'), b.val('M'))
                    s.wait_ge(b.sem('V'), b.val('V'))
                    b.inc(s.dma_start(out=stg,
                                      in_=w_dram[oc * 128:(oc + 1) * 128, :]),
                          'Lx', 16)

                for dsub in range(NCH):
                    @block.tensor
                    def _(t, oc=oc, dsub=dsub):
                        t.wait_ge(b.sem('Lx'), b.val('Lx'))
                        t.wait_ge(b.sem('V'), b.val('V'))
                        b.inc(t.transpose(pb[6][:, 0:128],
                                          stg[:, dsub * 128:(dsub + 1) * 128],
                                          cm['idf'][:]), 'M')

                    @block.vector
                    def _(v, oc=oc, dsub=dsub):
                        v.wait_ge(b.sem('M'), b.val('M'))
                        b.inc(v.tensor_copy(
                            wt[:, dsub * D + oc * 128: dsub * D + (oc + 1) * 128],
                            pb[6][:, 0:128]), 'V')

        emit_wT(dr['Wq_w'])
        if upto == 'wq':
            fin_()
            return nc
        NG = 1 if upto == 'qproj1' else NCH
        for g in range(NG):
            for ic in range(4):
                @block.tensor
                def _(t, g=g, ic=ic):
                    b.w(t, 'V', 'E')
                    for dc in range(NCH):
                        ins = t.matmul(pb[ic % 2][:],
                                       wt[:, dc * D + g * 128: dc * D + (g + 1) * 128],
                                       xnb[:, dc * T + ic * 512: dc * T + (ic + 1) * 512],
                                       start=(dc == 0), stop=(dc == NCH - 1))
                    b.inc(ins, 'M')

                @block.scalar
                def _(s, g=g, ic=ic):
                    s.wait_ge(b.sem('M'), b.val('M'))
                    b.inc(s.activation(
                        qop[:, g * T + ic * 512: g * T + (ic + 1) * 512],
                        pb[ic % 2][:], AF.Relu, bias=qbc[:, g:g + 1],
                        scale=0.125), 'E')
        if upto == 'qproj1':
            fin_()
            return nc

        if upto in ('qproj',):
            fin_()
            return nc

        @block.sync
        def _(s):
            b.w(s, 'V', 'P', 'E')
            b.inc(s.dma_start(out=vbrow[:],
                              in_=dr['Wv_b'].rearrange("(o f) -> o f", o=1)),
                  'L', 16)

        @block.vector
        def _(v):
            b.w(v, 'L')
            v.tensor_copy(vbh[:], vbrow[:])
            v.tensor_sub(vbrow[:], vbrow[:], vbh[:])
            b.inc(v.tensor_copy(vbl[:], vbrow[:]), 'V')

        for src_dram, wdram, bcol, scr, tokmaj in [
                (dr['keys'], dr['Wk_w'], kbc, kscr, False),
                (dr['values'], dr['Wv_w'], None, vscr, True)]:
            emit_wT(wdram)
            for kh in range(2):
                stg2 = rf.at(1024, 1024)
                for tt in range(16):
                    @block.sync
                    def _(s, tt=tt, kh=kh):
                        s.wait_ge(b.sem('M'), b.val('M'))
                        b.inc(s.dma_start(
                            out=stg2,
                            in_=src_dram[kh * T + tt * 128: kh * T + (tt + 1) * 128, :]),
                            'Lx', 16)

                    for c in range(NCH):
                        @block.tensor
                        def _(t, tt=tt, c=c):
                            t.wait_ge(b.sem('Lx'), b.val('Lx'))
                            t.wait_ge(b.sem('V'), b.val('V'))
                            b.inc(t.transpose(pb[6][:, 0:128],
                                              stg2[:, c * 128:(c + 1) * 128],
                                              cm['idf'][:]), 'M')

                        @block.vector
                        def _(v, tt=tt, c=c):
                            v.wait_ge(b.sem('M'), b.val('M'))
                            b.inc(v.tensor_copy(
                                fmb[:, c * T + tt * 128: c * T + (tt + 1) * 128],
                                pb[6][:, 0:128]), 'V')

                if not tokmaj:
                    for g in range(NCH):
                        for ic in range(4):
                            @block.tensor
                            def _(t, g=g, ic=ic):
                                b.w(t, 'V', 'E')
                                for dc in range(NCH):
                                    ins = t.matmul(pb[2 + ic % 2][:],
                                             wt[:, dc * D + g * 128: dc * D + (g + 1) * 128],
                                             fmb[:, dc * T + ic * 512: dc * T + (ic + 1) * 512],
                                             start=(dc == 0), stop=(dc == NCH - 1))
                                b.inc(ins, 'M')

                            @block.scalar
                            def _(s, g=g, ic=ic):
                                s.wait_ge(b.sem('M'), b.val('M'))
                                s.wait_ge(b.sem('S'), b.val('S'))
                                b.inc(s.activation(kst[:, (ic % 2) * 512:(ic % 2) * 512 + 512],
                                                   pb[2 + ic % 2][:], AF.Relu,
                                                   bias=kbc[:, g:g + 1]), 'E')

                            @block.sync
                            def _(s, g=g, ic=ic, kh=kh):
                                s.wait_ge(b.sem('E'), b.val('E'))
                                b.inc(s.dma_start(
                                    out=kscr[g * 128:(g + 1) * 128,
                                             kh * T + ic * 512: kh * T + (ic + 1) * 512],
                                    in_=kst[:, (ic % 2) * 512:(ic % 2) * 512 + 512]),
                                    'S', 16)
                else:
                    for tt in range(16):
                        for oh in range(2):
                            @block.tensor
                            def _(t, tt=tt, oh=oh):
                                b.w(t, 'V', 'E')
                                for dc in range(NCH):
                                    t.matmul(pb[4 + oh][:],
                                             fmb[:, dc * T + tt * 128: dc * T + (tt + 1) * 128],
                                             wt[:, dc * D + oh * 512: dc * D + oh * 512 + 512],
                                             start=(dc == 0), stop=False)
                                t.matmul(pb[4 + oh][:], cm['ones1b'][:],
                                         vbh[:, oh * 512:(oh + 1) * 512],
                                         start=False, stop=False)
                                ins = t.matmul(pb[4 + oh][:], cm['ones1b'][:],
                                               vbl[:, oh * 512:(oh + 1) * 512],
                                               start=False, stop=True)
                                b.inc(ins, 'M')

                            @block.scalar
                            def _(s, tt=tt, oh=oh):
                                s.wait_ge(b.sem('M'), b.val('M'))
                                s.wait_ge(b.sem('S'), b.val('S'))
                                b.inc(s.activation(kst[:, oh * 512:(oh + 1) * 512],
                                                   pb[4 + oh][:], AF.Relu), 'E')

                            @block.sync
                            def _(s, tt=tt, oh=oh, kh=kh):
                                s.wait_ge(b.sem('E'), b.val('E'))
                                b.inc(s.dma_start(
                                    out=vscr[kh * T + tt * 128: kh * T + (tt + 1) * 128,
                                             oh * 512:(oh + 1) * 512],
                                    in_=kst[:, oh * 512:(oh + 1) * 512]), 'S', 16)

        if upto == 'proj':
            fin_()
            return nc

        khb = rbf.at(KH, 4096)
        vhb = [rbf.at(KH + 4096 + i * 2080, 2080) for i in range(2)]
        ptb = rbf.at(4096, 2 * 512)
        oun = rf.at(0, 512, 65)
        rrow = rf.at(512, 512, 1)
        oth = rf.at(1024, 2048, 64)

        @block.vector
        def _(v):
            b.w(v, 'S', 'E')
            v.memset(vhb[0][:], 1.0)
            b.inc(v.memset(vhb[1][:], 1.0), 'V')

        for h in range(16):
            @block.gpsimd
            def _(g, h=h):
                if h >= 2:
                    g.wait_ge(b.sem('M'), b.val('M'))
                else:
                    b.w(g, 'S', 'V')
                hrows = khb[(h % 2) * 64:(h % 2) * 64 + 64, :]
                b.inc(g.dma_start(out=hrows,
                                  in_=kscr[h * 64:(h + 1) * 64, :]), 'G', 16)
                for jg in range(8):
                    b.inc(g.dma_start(
                        out=vhb[h % 2][:].rearrange("p (j c) -> p j c", j=32)
                        [:, jg * 4:(jg + 1) * 4, 0:64],
                        in_=vscr[jg * 512:(jg + 1) * 512, h * 64:(h + 1) * 64]
                        .rearrange("(j p) c -> p j c", p=128)), 'G', 16)

            for ic in range(4):
                def sc_mm(t, jt, h=h, ic=ic):
                    t.wait_ge(b.sem('E'), b.val('E'))
                    t.wait_ge(b.sem('G'), b.val('G'))
                    r0 = (h % 2) * 64
                    ins = t.matmul(pb[jt % 2][:],
                                   khb[r0:r0 + 64, jt * 128:(jt + 1) * 128],
                                   qop[r0:r0 + 64,
                                       (h // 2) * T + ic * 512:
                                       (h // 2) * T + (ic + 1) * 512],
                                   start=True, stop=True)
                    b.inc(ins, 'M')

                def ex_op(s, jt):
                    s.wait_ge(b.sem('M'), b.val('M'))
                    b.inc(s.activation(ptb[:, (jt % 2) * 512:(jt % 2) * 512 + 512],
                                       pb[jt % 2][:], AF.Exp), 'E')

                def pv_mm(t, jt, h=h):
                    t.wait_ge(b.sem('E'), b.val('E'))
                    ins = t.matmul(pb[2][0:65, :],
                             vhb[h % 2][:].rearrange("p (j c) -> p j c", j=32)[:, jt, :],
                             ptb[:, (jt % 2) * 512:(jt % 2) * 512 + 512],
                             start=(jt == 0), stop=(jt == 31))
                    b.inc(ins, 'M')

                for jt in range(32):
                    if jt >= 1:
                        @block.scalar
                        def _(s, jt=jt):
                            ex_op(s, jt - 1)

                    @block.tensor
                    def _(t, jt=jt):
                        sc_mm(t, jt)
                        if jt >= 1:
                            pv_mm(t, jt - 1)

                @block.scalar
                def _(s):
                    ex_op(s, 31)

                @block.tensor
                def _(t):
                    pv_mm(t, 31)

                @block.scalar
                def _(s, h=h, ic=ic):
                    s.wait_ge(b.sem('M'), b.val('M'))
                    s.wait_ge(b.sem('V'), b.val('V'))
                    s.copy(oun[:], pb[2][0:65, :])
                    s.activation(rrow[:], oun[64:65, :], AF.Ln)
                    b.inc(s.activation(rrow[:], rrow[:], AF.Exp, scale=-1.0),
                          'E')

                @block.vector
                def _(v, h=h, ic=ic):
                    b.w(v, 'E')
                    v.wait_ge(b.sem('M'), b.val('M'))
                    b.inc(emit_split_row(b, v, rrow[:], 512, cm), 'V')

                @block.tensor
                def _(t, h=h, ic=ic):
                    b.w(t, 'E', 'V')
                    b.inc(emit_bcast_mm(b, t, pb[3][0:64, :], 512, cm, m=64), 'M')

                @block.vector
                def _(v, h=h, ic=ic):
                    v.wait_ge(b.sem('M'), b.val('M'))
                    b.inc(v.tensor_tensor(oth[0:64, ic * 512:(ic + 1) * 512],
                                          oun[0:64, :], pb[3][0:64, :],
                                          ALU.mult), 'V')

            for q in range(2):
                for vv in range(8):
                    @block.tensor
                    def _(t, h=h, q=q, vv=vv):
                        b.w(t, 'V')
                        t.wait_ge(b.sem('P'), b.val('P'))
                        b.inc(t.transpose(pb[4][:, 0:64],
                                          oth[0:64, 1024 * q + 128 * vv:
                                              1024 * q + 128 * (vv + 1)],
                                          cm['idf'][0:64, 0:64]), 'M')

                    @block.vector
                    def _(v, h=h, q=q, vv=vv):
                        v.wait_ge(b.sem('M'), b.val('M'))
                        tgt = _scramble_slice(xs, vv, h, q)
                        b.inc(v.tensor_add(tgt, tgt, pb[4][:, 0:64]), 'V')

        if last:
            emit_stream_out_tok(b, block, xs, dr['xo'], cm,
                                wait=('V', 'P', 'E', 'S'))
        else:
            emit_stream_out_fm(b, block, xs, dr['xo'], wait=('V', 'P', 'E', 'S'))

    return nc


def _scramble_slice(xs, vv, h, q):
    full = xs[:, vv * CW + 1: vv * CW + 1 + T].rearrange(
        "p (d two) -> p d two", two=2)
    return full[:, h * 64:(h + 1) * 64, q:q + 1]




def build_ffn0(first=False, last=False, upto='all'):
    nc = bass.Bass("TRN2", target_bir_lowering=False)
    ctx = ExitStack()
    b = Bld(nc, ctx)
    HID = 2730
    NFC = 22

    dr = {}
    dr['xi'] = nc.dram_tensor("xi", [T, D] if first else [D, T], F32,
                              kind="ExternalInput").ap()
    dr['xo'] = nc.dram_tensor("xo", [T, D] if last else [D, T], F32,
                              kind="ExternalOutput").ap()
    for nm, shp in [('wqkv', [2, D, 1536]), ('wo', [2, 512, D]),
                    ('w1', [2, D, 2 * HID]), ('w2', [2, HID, D]),
                    ('q_scale', [2, 64]), ('k_scale', [2, 64]),
                    ('ln_g', [2, D]), ('ln_b', [2, D]),
                    ('b1', [2, 2 * HID]), ('b2', [2, D])]:
        dr[nm] = nc.dram_tensor(nm, shp, F32, kind="ExternalInput").ap()
    b.idf_dram = nc.dram_tensor("idf", [128, 128], F32, kind="ExternalInput").ap()
    dr['rotc'] = nc.dram_tensor("rotc", [128, 2 * T], BF16,
                                kind="ExternalInput").ap()
    dr['rots'] = nc.dram_tensor("rots", [128, 2 * T], BF16,
                                kind="ExternalInput").ap()
    dr['sel64'] = nc.dram_tensor("sel64", [128, 2], BF16,
                                 kind="ExternalInput").ap()
    dr['sel64t'] = nc.dram_tensor("sel64t", [2, 128], BF16,
                                  kind="ExternalInput").ap()
    dr['perm32'] = nc.dram_tensor("perm32", [128, 128], BF16,
                                  kind="ExternalInput").ap()
    xscr = nc.dram_tensor("xscr", [D, T], F32, kind="Internal").ap()
    vscr = nc.dram_tensor("vscr", [T, 512], BF16, kind="Internal").ap()

    with ctx:
        block = ctx.enter_context(nc.Block())
        cm = make_common(b, block, nbf=49700, nf32=5696)
        pb = cm['pb']
        rf, rbf = cm['rf'], cm['rbf']
        zs = b.sb("zs", [128, NCH * CW], F32)
        rotc = b.sb("rotc", [128, 2 * T], BF16)
        rots = b.sb("rots", [128, 2 * T], BF16)
        sel64 = b.sb("sel64", [128, 2], BF16)
        sel64t = b.sb("sel64t", [2, 128], BF16)
        perm32 = b.sb("perm32", [128, 128], BF16)
        qsc = b.sb("qsc", [128, 2], F32)
        ksc = b.sb("ksc", [128, 2], F32)
        lgc = b.sb("lgc", [128, 16], F32)
        lbc = b.sb("lbc", [128, 16], F32)
        b2c = b.sb("b2c", [128, 16], F32)
        b1c = b.sb("b1c", [128, 44], F32)
        g1c = b.sb("g1c", [128, 44], F32)
        e24 = b.sb("e24", [128, 1], F32)

        ZB, QOP, KOP, OALL = 0, 16384, 24576, 32768
        WBUF, PTB = 40960, 45056
        SC1, SC2 = 46080, 46592
        VH = 47616
        zb = rbf.at(ZB, NCH * T)
        qop = rbf.at(QOP, 4 * T)
        kop = rbf.at(KOP, 4 * T)
        oall = rbf.at(OALL, 4 * T)
        wbuf = rbf.at(WBUF, 4096)
        ptb = rbf.at(PTB, 2 * 512)
        sc1 = rbf.at(SC1, 512)
        sc2 = rbf.at(SC2, 512)
        vh = [rbf.at(VH + i * 1040, 1040) for i in range(2)]
        h1s = rbf.at(QOP, 11264)

        lnscr = {
            'xb2': rbf.at(QOP, 2 * T), 'sq2': rbf.at(QOP + 4096, 2 * T),
            'nt': rf.at(0, 1024), 'bcR': rf.at(1024, 1024),
            'bcM': rf.at(2048, 1024),
            'rows': [rf.at(3072 + i * 512, 512, 1) for i in range(4)],
        }

        @block.sync
        def _(s):
            for t_, d_ in [(rotc, dr['rotc']), (rots, dr['rots']),
                           (sel64, dr['sel64']), (sel64t, dr['sel64t']),
                           (perm32, dr['perm32'])]:
                b.inc(s.dma_start(out=t_[:], in_=d_), 'L', 16)

        @block.vector
        def _(v):
            b.inc(v.memset(e24[:], 1e-24), 'V')

        @block.sync
        def _(s):
            for l in range(2):
                for hf in range(2):
                    s.dma_start(
                        out=qsc[hf * 64:(hf + 1) * 64, l:l + 1],
                        in_=dr['q_scale'][l].rearrange("(p o) -> p o", o=1)
                    ).then_inc(b.sem('Lc'), 16)
                    b.cnt['Lc'] += 16
                    ins = s.dma_start(
                        out=ksc[hf * 64:(hf + 1) * 64, l:l + 1],
                        in_=dr['k_scale'][l].rearrange("(p o) -> p o", o=1))
                    b.inc(ins, 'Lc', 16)

        @block.vector
        def _(v):
            v.wait_ge(b.sem('Lc'), b.val('Lc'))
            ins = v.tensor_scalar_mul(qsc[:], qsc[:], 8.0)
            b.inc(ins, 'V')

        @block.vector
        def _(v):
            v.wait_ge(b.sem('V'), b.val('V'))
            b.inc(v.tensor_copy(ksc[:], ksc[:]), 'V')
        emit_cols(b, block, dr['ln_g'].rearrange("l d -> () (l d)"), lgc, 16, cm)
        emit_cols(b, block, dr['ln_b'].rearrange("l d -> () (l d)"), lbc, 16, cm)
        emit_cols(b, block, dr['b2'].rearrange("l d -> () (l d)"), b2c, 16, cm)
        for l in range(2):
            emit_cols(b, block, dr['b1'][l][0:2730]
                      .rearrange("(o f) -> o f", o=1), b1c[:, l * 22:],
                      22, cm, nlast=42)
            emit_cols(b, block, dr['b1'][l][HID:HID + 2730]
                      .rearrange("(o f) -> o f", o=1), g1c[:, l * 22:],
                      22, cm, nlast=42)

        if first:
            emit_stream_in_tok(b, block, dr['xi'], zs, cm)
        else:
            emit_stream_in_fm(b, block, dr['xi'], zs)

        @block.sync
        def _(s):
            b.w(s, 'L', 'V')
            for c in range(NCH):
                b.inc(s.dma_start(out=xscr[c * 128:(c + 1) * 128, :],
                                  in_=zs[:, c * CW + 1:c * CW + 1 + T]), 'S', 16)

        emit_ln(b, block, zs, zb, cm, lnscr, wait=('L', 'V', 'M', 'S'),
                f32_inplace=True)

        packf = dict(zs=zs, zb=zb, qop=qop, kop=kop, oall=oall, wbuf=wbuf,
                     h1s=h1s, ptb=ptb, sc1=sc1, sc2=sc2, vh=vh, qsc=qsc,
                     ksc=ksc, lgc=lgc, lbc=lbc, b1c=b1c, g1c=g1c, b2c=b2c,
                     e24=e24, sel64=sel64, sel64t=sel64t, perm32=perm32,
                     rotc=rotc, rots=rots, vscr=vscr, lnscr=lnscr)
        nlayers = 2 if upto == 'all' else 1
        for l in range(nlayers):
            layer_ffn0(b, block, cm, dr, l, packf, upto)

        xstg = rf.at(0, 1024)
        for c in range(NCH):
            for ih in range(2):
                @block.sync
                def _(s, c=c, ih=ih):
                    s.wait_ge(b.sem('V'), b.val('V'))
                    if c == 0 and ih == 0:
                        b.w(s, 'E', 'M', 'P')
                    b.inc(s.dma_start(out=xstg,
                                      in_=xscr[c * 128:(c + 1) * 128,
                                               ih * 1024:(ih + 1) * 1024]),
                          'Lx', 16)

                @block.vector
                def _(v, c=c, ih=ih):
                    v.wait_ge(b.sem('Lx'), b.val('Lx'))
                    zd = zs[:, c * CW + 1 + ih * 1024: c * CW + 1 + (ih + 1) * 1024]
                    b.inc(v.scalar_tensor_tensor(zd, zd, 0.5, xstg,
                                                 ALU.mult, ALU.add), 'V')

        if last:
            emit_stream_out_tok(b, block, zs, dr['xo'], cm,
                                wait=('V', 'P', 'E', 'S'))
        else:
            emit_stream_out_fm(b, block, zs, dr['xo'], wait=('V', 'P', 'E', 'S'))

    return nc


def layer_ffn0(b, block, cm, dr, l, P, upto):
    pb = cm['pb']
    rf = cm['rf']
    zs, zb = P['zs'], P['zb']
    qop, kop, oall = P['qop'], P['kop'], P['oall']
    wbuf, h1s, ptb = P['wbuf'], P['h1s'], P['ptb']
    sc1, sc2, vh = P['sc1'], P['sc2'], P['vh']
    vscr = P['vscr']
    HID = 2730
    NFC = 22

    for c in range(NCH):
        @block.gpsimd
        def _(g, c=c):
            if c == 0:
                b.w(g, 'V', 'E', 'M')
            off = 1 if c < 4 else 0
            b.inc(g.tensor_copy(zb[:, c * T:(c + 1) * T],
                                zs[:, c * CW + off: c * CW + off + T]), 'P')

    if upto == 'A':
        return
    rA = rf.at(3072, 512, 2)
    rB = rf.at(3584, 512, 2)
    rT1 = rf.at(0, 512)
    rT2 = rf.at(512, 512)
    h2r = rf.at(4096, 512, 2)
    for side in (0, 1):
        dst = qop if side == 0 else kop
        scol = P['qsc'] if side == 0 else P['ksc']

        @block.gpsimd
        def _(g, side=side):
            g.wait_ge(b.sem('M'), b.val('M'))
            g.wait_ge(b.sem('P'), b.val('P'))
            for dc in range(NCH):
                b.inc(g.dma_start(
                    out=wbuf[:, dc * 512:(dc + 1) * 512],
                    in_=dr['wqkv'][l][dc * 128:(dc + 1) * 128,
                                      side * 512:(side + 1) * 512]), 'G', 16)

        for g_ in range(4):
            for ic in range(4):
                @block.tensor
                def _(t, g_=g_, ic=ic, side=side):
                    b.w(t, 'G', 'P', 'V', 'E')
                    for dc in range(NCH):
                        ins = t.matmul(pb[ic % 2][:],
                                       wbuf[:, dc * 512 + g_ * 128:
                                            dc * 512 + (g_ + 1) * 128],
                                       zb[:, dc * T + ic * 512:
                                          dc * T + (ic + 1) * 512],
                                       start=(dc == 0), stop=(dc == NCH - 1))
                    b.inc(ins, 'M')

                sl = lambda buf, g_=g_, ic=ic: buf[:, g_ * T + ic * 512:
                                                   g_ * T + (ic + 1) * 512]

                @block.scalar
                def _(s, g_=g_, ic=ic, side=side, dst=dst):
                    s.wait_ge(b.sem('M'), b.val('M'))
                    s.wait_ge(b.sem('V'), b.val('V'))
                    s.copy(sl(dst), pb[ic % 2][:])
                    b.inc(s.activation(sc1[:], pb[ic % 2][:], AF.Square), 'E')

                @block.tensor
                def _(t, g_=g_, ic=ic):
                    t.wait_ge(b.sem('E'), b.val('E'))
                    b.inc(t.matmul(pb[6][0:2, 0:512], P['sel64'][:], sc1[:],
                                   start=True, stop=True), 'M')

                @block.vector
                def _(v, g_=g_, ic=ic):
                    v.wait_ge(b.sem('M'), b.val('M'))
                    b.inc(v.tensor_scalar(rA[:], pb[6][0:2, 0:512],
                                          P['e24'][0:2, 0:1], None, ALU.max),
                          'V')

                @block.scalar
                def _(s, g_=g_, ic=ic):
                    s.wait_ge(b.sem('V'), b.val('V'))
                    s.activation(rB[:], rA[:], AF.Ln)
                    b.inc(s.activation(rB[:], rB[:], AF.Exp, scale=-0.5), 'E')

                @block.vector
                def _(v, g_=g_, ic=ic):
                    v.wait_ge(b.sem('E'), b.val('E'))
                    v.tensor_copy(sc2[0:2, 0:512], rB[:])
                    v.tensor_sub(h2r[:], rB[:], sc2[0:2, 0:512])
                    b.inc(v.tensor_copy(sc1[0:2, 0:512], h2r[:]), 'V')

                @block.tensor
                def _(t, g_=g_, ic=ic):
                    t.wait_ge(b.sem('V'), b.val('V'))
                    t.matmul(pb[7][:], P['sel64t'][:], sc2[0:2, 0:512],
                             start=True, stop=False)
                    b.inc(t.matmul(pb[7][:], P['sel64t'][:],
                                   sc1[0:2, 0:512], start=False, stop=True),
                          'M')

                @block.vector
                def _(v, g_=g_, ic=ic, side=side, dst=dst):
                    v.wait_ge(b.sem('M'), b.val('M'))
                    b.inc(v.scalar_tensor_tensor(sc1[:], sl(dst),
                                                 scol[:, l:l + 1], pb[7][:],
                                                 ALU.mult, ALU.mult), 'V')

                @block.tensor
                def _(t, g_=g_, ic=ic):
                    t.wait_ge(b.sem('V'), b.val('V'))
                    b.inc(t.matmul(pb[6][:], P['perm32'][:], sc1[:],
                                   start=True, stop=True), 'M')

                @block.vector
                def _(v, g_=g_, ic=ic, side=side, dst=dst):
                    v.wait_ge(b.sem('M'), b.val('M'))
                    rc = P['rotc'][:, side * T + ic * 512: side * T + (ic + 1) * 512]
                    rs = P['rots'][:, side * T + ic * 512: side * T + (ic + 1) * 512]
                    v.tensor_tensor(rT1[:], sc1[:], rc, ALU.mult)
                    v.scalar_tensor_tensor(rT2[:], pb[6][:], 0.0, rs,
                                           ALU.bypass, ALU.mult)
                    b.inc(v.tensor_add(sl(dst), rT1[:], rT2[:]), 'V')

    if upto == 'B':
        return
    @block.gpsimd
    def _(g):
        g.wait_ge(b.sem('M'), b.val('M'))
        for dc in range(NCH):
            b.inc(g.dma_start(
                out=wbuf[:, dc * 512:(dc + 1) * 512],
                in_=dr['wqkv'][l][dc * 128:(dc + 1) * 128, 1024:1536]), 'G', 16)

    for tt in range(16):
        @block.tensor
        def _(t, tt=tt):
            b.w(t, 'G', 'P', 'V', 'E')
            t.wait_ge(b.sem('S'), b.val('S'))
            for dc in range(NCH):
                ins = t.matmul(pb[4 + tt % 2][:],
                               zb[:, dc * T + tt * 128: dc * T + (tt + 1) * 128],
                               wbuf[:, dc * 512:(dc + 1) * 512],
                               start=(dc == 0), stop=(dc == NCH - 1))
            b.inc(ins, 'M')

        @block.scalar
        def _(s, tt=tt):
            s.wait_ge(b.sem('M'), b.val('M'))
            s.wait_ge(b.sem('S'), b.val('S'))
            b.inc(s.copy(sc1[:], pb[4 + tt % 2][:]), 'E')

        @block.sync
        def _(s, tt=tt):
            s.wait_ge(b.sem('E'), b.val('E'))
            b.inc(s.dma_start(out=vscr[tt * 128:(tt + 1) * 128, :],
                              in_=sc1[:]), 'S', 16)

    if upto == 'Bv':
        return
    @block.vector
    def _(v):
        b.w(v, 'E')
        v.memset(vh[0][:], 1.0)
        b.inc(v.memset(vh[1][:], 1.0), 'V')

    for h in range(1 if upto in ('Ch0', 'Cw0', 'Cnn') else 8):
        r0 = (h % 2) * 64
        g_ = h // 2

        @block.gpsimd
        def _(g, h=h):
            g.wait_ge(b.sem('M'), b.val('M'))
            g.wait_ge(b.sem('V'), b.val('V'))
            if h < 2:
                g.wait_ge(b.sem('S'), b.val('S'))
            for jg in range(4):
                b.inc(g.dma_start(
                    out=vh[h % 2][:].rearrange("p (j c) -> p j c", j=16)
                    [:, jg * 4:(jg + 1) * 4, 0:64],
                    in_=vscr[jg * 512:(jg + 1) * 512, h * 64:(h + 1) * 64]
                    .rearrange("(j p) c -> p j c", p=128)), 'G', 16)

        for w in range(1 if upto == 'Cw0' else 4):
            jts = list(range(4)) if w == 0 else \
                [4 * (w - 1) + k for k in range(8)]
            i0 = w * 512

            def msk(gg, jt, i0=i0):
                d0 = i0 - jt * 128
                sel = []
                if d0 < 0 or d0 - 127 < 0:
                    sel.append((d0, -1, 1))
                if d0 + 511 > 512:
                    sel.append((512 - d0, 1, -1))
                return sel

            def sc_mm(t, jt, h=h, w=w, r0=r0, g_=g_):
                t.wait_ge(b.sem('E'), b.val('E'))
                t.wait_ge(b.sem('G'), b.val('G'))
                t.wait_ge(b.sem('P'), b.val('P'))
                ins = t.matmul(pb[jt % 2][:],
                               kop[r0:r0 + 64, g_ * T + jt * 128:
                                   g_ * T + (jt + 1) * 128],
                               qop[r0:r0 + 64, g_ * T + w * 512:
                                   g_ * T + (w + 1) * 512],
                               start=True, stop=True)
                b.inc(ins, 'M')

            def ex_op(s, jt):
                s.wait_ge(b.sem('M'), b.val('M'))
                b.inc(s.activation(ptb[:, (jt % 2) * 512:(jt % 2) * 512 + 512],
                                   pb[jt % 2][:], AF.Exp), 'E')

            def mask_op(g, jt, w=w):
                sels = msk(g, jt)
                for base, cmul, step in sels:
                    g.wait_ge(b.sem('E'), b.val('E'))
                    sl_ = ptb[:, (jt % 2) * 512:(jt % 2) * 512 + 512]
                    b.inc(g.affine_select(sl_, sl_, pattern=[[step, 512]],
                                          compare_op=ALU.is_ge, fill=0.0,
                                          base=base,
                                          channel_multiplier=cmul), 'P')

            def pv_mm(t, idx, jt, h=h):
                t.wait_ge(b.sem('E'), b.val('E'))
                t.wait_ge(b.sem('P'), b.val('P'))
                ins = t.matmul(pb[2][0:65, :],
                               vh[h % 2][:].rearrange("p (j c) -> p j c", j=16)
                               [:, jt, :],
                               ptb[:, (jt % 2) * 512:(jt % 2) * 512 + 512],
                               start=(idx == 0), stop=(idx == len(jts) - 1))
                b.inc(ins, 'M')

            for idx, jt in enumerate(jts):
                if idx >= 1:
                    @block.scalar
                    def _(s, jt=jts[idx - 1]):
                        ex_op(s, jt)

                    @block.gpsimd
                    def _(g, jt=jts[idx - 1]):
                        mask_op(g, jt)

                @block.tensor
                def _(t, idx=idx, jt=jt):
                    sc_mm(t, jt)
                    if idx >= 1:
                        pv_mm(t, idx - 1, jts[idx - 1])

            @block.scalar
            def _(s):
                ex_op(s, jts[-1])

            @block.gpsimd
            def _(g):
                mask_op(g, jts[-1])

            @block.tensor
            def _(t):
                pv_mm(t, len(jts) - 1, jts[-1])

            if upto == 'Cnn':
                continue
            oun = rf.at(0, 512, 65)
            rrow = rf.at(512, 512, 1)

            @block.scalar
            def _(s, h=h, w=w):
                s.wait_ge(b.sem('M'), b.val('M'))
                s.wait_ge(b.sem('V'), b.val('V'))
                s.copy(oun[:], pb[2][0:65, :])
                s.activation(rrow[:], oun[64:65, :], AF.Ln)
                b.inc(s.activation(rrow[:], rrow[:], AF.Exp, scale=-1.0), 'E')

            @block.vector
            def _(v, h=h, w=w):
                b.w(v, 'E')
                v.wait_ge(b.sem('M'), b.val('M'))
                b.inc(emit_split_row(b, v, rrow[:], 512, cm), 'V')

            @block.tensor
            def _(t, h=h, w=w):
                b.w(t, 'E', 'V')
                b.inc(emit_bcast_mm(b, t, pb[3][0:64, :], 512, cm, m=64), 'M')

            @block.vector
            def _(v, h=h, w=w, r0=r0, g_=g_):
                v.wait_ge(b.sem('M'), b.val('M'))
                b.inc(v.tensor_tensor(
                    oall[r0:r0 + 64, g_ * T + w * 512: g_ * T + (w + 1) * 512],
                    oun[0:64, :], pb[3][0:64, :], ALU.mult), 'V')

    if upto in ('C', 'Ch0', 'Cw0', 'Cnn'):
        return
    @block.gpsimd
    def _(g):
        g.wait_ge(b.sem('M'), b.val('M'))
        for icc in range(4):
            b.inc(g.dma_start(out=wbuf[:, icc * 1024:(icc + 1) * 1024],
                              in_=dr['wo'][l][icc * 128:(icc + 1) * 128, :]),
                  'G', 16)

    for oc in range(NCH):
        for ic in range(4):
            @block.tensor
            def _(t, oc=oc, ic=ic):
                b.w(t, 'G', 'V', 'E')
                for icc in range(4):
                    ins = t.matmul(pb[4 + (oc * 4 + ic) % 2][:],
                                   wbuf[:, icc * 1024 + oc * 128:
                                        icc * 1024 + (oc + 1) * 128],
                                   oall[:, icc * T + ic * 512:
                                        icc * T + (ic + 1) * 512],
                                   start=(icc == 0), stop=(icc == 3))
                b.inc(ins, 'M')

            @block.vector
            def _(v, oc=oc, ic=ic):
                v.wait_ge(b.sem('M'), b.val('M'))
                zd = zs[:, oc * CW + 1 + ic * 512: oc * CW + 1 + (ic + 1) * 512]
                b.inc(v.tensor_add(zd, zd, pb[4 + (oc * 4 + ic) % 2][:]), 'V')

    if upto == 'D':
        return
    emit_ln(b, block, zs, zb, cm, P['lnscr'], shift=True,
            gain_col=P['lgc'][:, l * 8:(l + 1) * 8],
            bias_col=P['lbc'][:, l * 8:(l + 1) * 8], wait=('V', 'E', 'M', 'P'))

    if upto == 'E':
        return
    for ic in range(4):
        for fc in range(NFC):
            fr = 128 if fc < NFC - 1 else 42

            @block.gpsimd
            def _(g, ic=ic, fc=fc, fr=fr):
                g.wait_ge(b.sem('M'), b.val('M'))
                b.inc(g.dma_start(
                    out=wbuf[:, 0:NCH * 128].rearrange("p (dc f) -> p dc f", dc=NCH)
                    [:, :, 0:fr],
                    in_=dr['w1'][l][:, fc * 128:fc * 128 + fr]
                    .rearrange("(dc p) f -> p dc f", p=128)), 'G', 16)
                b.inc(g.dma_start(
                    out=wbuf[:, NCH * 128:2 * NCH * 128]
                    .rearrange("p (dc f) -> p dc f", dc=NCH)[:, :, 0:fr],
                    in_=dr['w1'][l][:, HID + fc * 128: HID + fc * 128 + fr]
                    .rearrange("(dc p) f -> p dc f", p=128)), 'G', 16)

            @block.tensor
            def _(t, ic=ic, fc=fc, fr=fr):
                t.wait_ge(b.sem('G'), b.val('G'))
                t.wait_ge(b.sem('V'), b.val('V'))
                t.wait_ge(b.sem('E'), b.val('E'))
                for dc in range(NCH):
                    insa = t.matmul(pb[0][0:fr, :],
                                    wbuf[:, dc * 128:dc * 128 + fr],
                                    zb[:, dc * T + ic * 512: dc * T + (ic + 1) * 512],
                                    start=(dc == 0), stop=(dc == NCH - 1))
                for dc in range(NCH):
                    insg = t.matmul(pb[1][0:fr, :],
                                    wbuf[:, NCH * 128 + dc * 128:
                                         NCH * 128 + dc * 128 + fr],
                                    zb[:, dc * T + ic * 512: dc * T + (ic + 1) * 512],
                                    start=(dc == 0), stop=(dc == NCH - 1))
                insa.then_inc(b.sem('M'), 1)
                b.cnt['M'] += 1
                b.inc(insg, 'M')

            @block.scalar
            def _(s, ic=ic, fc=fc, fr=fr):
                s.wait_ge(b.sem('M'), b.val('M'))
                s.wait_ge(b.sem('V'), b.val('V'))
                s.activation(sc1[0:fr, :], pb[0][0:fr, :], AF.Identity,
                             bias=P['b1c'][0:fr, l * 22 + fc:l * 22 + fc + 1])
                b.inc(s.activation(sc2[0:fr, :], pb[1][0:fr, :], AF.Gelu,
                                   bias=P['g1c'][0:fr, l * 22 + fc:
                                                 l * 22 + fc + 1]), 'E')

            @block.vector
            def _(v, ic=ic, fc=fc, fr=fr):
                v.wait_ge(b.sem('E'), b.val('E'))
                b.inc(v.tensor_tensor(h1s[0:fr, fc * 512:(fc + 1) * 512],
                                      sc1[0:fr, :], sc2[0:fr, :], ALU.mult),
                      'V')

        for ocp in range(4):
            for fc in range(NFC):
                fr = 128 if fc < NFC - 1 else 42

                @block.gpsimd
                def _(g, ic=ic, ocp=ocp, fc=fc, fr=fr):
                    g.wait_ge(b.sem('M'), b.val('M'))
                    b.inc(g.dma_start(
                        out=wbuf[:, 2048 + (fc % 2) * 256:
                                 2048 + (fc % 2) * 256 + 256][0:fr, :],
                        in_=dr['w2'][l][fc * 128:fc * 128 + fr,
                                        ocp * 256:(ocp + 1) * 256]), 'G', 16)

                @block.tensor
                def _(t, ic=ic, ocp=ocp, fc=fc, fr=fr):
                    t.wait_ge(b.sem('G'), b.val('G'))
                    if fc == 0:
                        t.wait_ge(b.sem('V'), b.val('V'))
                    w2t = wbuf[:, 2048 + (fc % 2) * 256:
                               2048 + (fc % 2) * 256 + 256][0:fr, :]
                    ins = t.matmul(pb[2][:], w2t[:, 0:128],
                                   h1s[0:fr, fc * 512:(fc + 1) * 512],
                                   start=(fc == 0), stop=(fc == NFC - 1))
                    ins2 = t.matmul(pb[3][:], w2t[:, 128:256],
                                    h1s[0:fr, fc * 512:(fc + 1) * 512],
                                    start=(fc == 0), stop=(fc == NFC - 1))
                    if fc == NFC - 1:
                        ins.then_inc(b.sem('M'), 1)
                        b.cnt['M'] += 1
                        b.inc(ins2, 'M')
                    else:
                        b.inc(ins2, 'M')

            @block.vector
            def _(v, ic=ic, ocp=ocp):
                v.wait_ge(b.sem('M'), b.val('M'))
                for k in range(2):
                    oc = 2 * ocp + k
                    zd = zs[:, oc * CW + 1 + ic * 512:
                            oc * CW + 1 + (ic + 1) * 512]
                    b.inc(v.scalar_tensor_tensor(
                        zd, pb[2 + k][:], P['b2c'][:, l * 8 + oc:l * 8 + oc + 1],
                        zd, ALU.add, ALU.add), 'V')



P1ORDER = [0, 0, 0, 1, 1, 2]
P2ORDER = [1, 2, 3, 2, 3, 3]


def build_attn0(first=False, last=False, upto='all'):
    nc = bass.Bass("TRN2", target_bir_lowering=False)
    ctx = ExitStack()
    b = Bld(nc, ctx)
    CWB = T + 8

    dr = {}
    dr['xi'] = nc.dram_tensor("xi", [T, D] if first else [D, T], F32,
                              kind="ExternalInput").ap()
    dr['xo'] = nc.dram_tensor("xo", [T, D] if last else [D, T], F32,
                              kind="ExternalOutput").ap()
    for nm, shp in [('qkv_w', [D, 3 * D]), ('w1_write', [D, 64]),
                    ('w2_write', [D, 64]), ('w1_read', [D, 64]),
                    ('w2_read', [D, 64]), ('memv_w', [D, D]),
                    ('memg_w', [D, 16]), ('out_w', [D, D]), ('J', [6, 6])]:
        dr[nm] = nc.dram_tensor(nm, shp, F32, kind="ExternalInput").ap()
    for nm, shp in [('qkv_b', [3 * D]), ('memv_b', [D]), ('memg_b', [16]),
                    ('mem_scale', [16]), ('out_b', [D]), ('decays', [16])]:
        dr[nm] = nc.dram_tensor(nm, shp, F32, kind="ExternalInput").ap()
    b.idf_dram = nc.dram_tensor("idf", [128, 128], F32, kind="ExternalInput").ap()
    dr['sel6'] = nc.dram_tensor("sel6", [96, 16], BF16, kind="ExternalInput").ap()
    dr['sel6t'] = nc.dram_tensor("sel6t", [16, 96], BF16,
                                 kind="ExternalInput").ap()
    dr['ehs'] = nc.dram_tensor("ehs", [128, 256], BF16,
                               kind="ExternalInput").ap()
    dr['psel'] = nc.dram_tensor("psel", [96, 512], BF16,
                                kind="ExternalInput").ap()
    dr['dio'] = nc.dram_tensor("dio", [128, 19], F32, kind="ExternalInput").ap()
    dr['rio'] = nc.dram_tensor("rio", [16, 512], F32, kind="ExternalInput").ap()
    xscr = nc.dram_tensor("xscr", [D, T], F32, kind="Internal").ap()
    vscr = nc.dram_tensor("vscr", [T, D], BF16, kind="Internal").ap()
    kscr = nc.dram_tensor("kscr", [D, T], BF16, kind="Internal").ap()
    qscr = nc.dram_tensor("qscr", [D, T], BF16, kind="Internal").ap()
    mvscr = nc.dram_tensor("mvscr", [D, T], BF16, kind="Internal").ap()

    with ctx:
        block = ctx.enter_context(nc.Block())
        cm = make_common(b, block, nbf=55400, nf32=5696)
        pb = cm['pb']
        rf, rbf = cm['rf'], cm['rbf']
        xs = b.sb("xs", [128, NCH * CW], F32)
        sel6 = b.sb("sel6", [96, 16], BF16)
        sel6t = b.sb("sel6t", [16, 96], BF16)
        psel = b.sb("psel", [96, 512], BF16)
        ehs = b.sb("ehs", [128, 256], BF16)
        dio = b.sb("dio", [128, 19], F32)
        jrep = b.sb("jrep", [96, 96], BF16)
        qbc = b.sb("qbc", [128, 8], F32)
        kbc = b.sb("kbc", [128, 8], F32)
        obc = b.sb("obc", [128, 8], F32)
        mvbc = b.sb("mvbc", [128, 8], F32)
        vbh2 = b.sb("vbh2", [1, D], BF16)
        vbl2 = b.sb("vbl2", [1, D], BF16)
        mgb = b.sb("mgb", [16, 1], F32)
        msc = b.sb("msc", [16, 1], F32)
        ldh = b.sb("ldh", [16, 1], F32)
        ldh2 = b.sb("ldh2", [16, 1], F32)
        ldrow = b.sb("ldrow", [1, 16], F32)
        ldcol = b.sb("ldcol", [128, 1], F32)
        e24b = b.sb("e24b", [128, 1], F32)

        xnb = rbf.at(0, NCH * CWB)
        wbuf = rbf.at(16448, 4096)
        vh = [rbf.at(16448 + i * 1040, 1040) for i in range(2)]
        ptb = rbf.at(18528, 1024)
        sc1 = rbf.at(19552, 512)
        sc2 = rbf.at(20064, 512)
        A1 = rbf.at(20544, 2048, 96)
        B1 = rbf.at(22592, 2048, 96)
        A2 = rbf.at(24640, 2048, 96)
        B2 = rbf.at(26688, 2048, 96)
        kh = [rbf.at(20544 + i * 2048, 2048, 64) for i in range(2)]
        qh = [rbf.at(24640 + i * 2048, 2048, 64) for i in range(2)]
        spr1 = rbf.at(20544, 2048)
        spr2 = rbf.at(22592, 2048)
        msq = rbf.at(24640, 512)
        sc1p = rbf.at(20544, 512)
        wl = rbf.at(28736, 2048, 96)
        sg = rbf.at(28736, 2048, 16)
        gate = rbf.at(30784, 2048, 16)
        gg = rbf.at(32832, 2048, 16)
        rl = rbf.at(34880, 2048, 96)
        jw = rbf.at(36928, 2048, 96)
        oto = rbf.at(38976, NCH * T)
        lnscr = {
            'xb2': rbf.at(16448, 2 * T), 'sq2': rbf.at(20544, 2 * T),
            'nt': rf.at(0, 1024), 'bcR': rf.at(1024, 1024),
            'bcM': rf.at(2048, 1024),
            'rows': [rf.at(3072 + i * 512, 512, 1) for i in range(4)],
        }
        mem = rf.at(0, 2048, 16)
        gmrow = rf.at(2048, 2048, 1)
        nrows = rf.at(4608, 512, 16)

        arow = rf.at(4096, 512, 16)

        @block.sync
        def _(s):
            for t_, d_ in [(sel6, dr['sel6']), (sel6t, dr['sel6t']),
                           (psel, dr['psel']), (ehs, dr['ehs']),
                           (dio, dr['dio'])]:
                b.inc(s.dma_start(out=t_[:], in_=d_), 'L', 16)

        @block.vector
        def _(v):
            b.inc(v.memset(e24b[:], 1e-24), 'V')

        emit_cols(b, block, dr['qkv_b'][0:1024].rearrange("(o f) -> o f", o=1),
                  qbc, 8, cm, scale=0.125)
        emit_cols(b, block, dr['qkv_b'][1024:2048]
                  .rearrange("(o f) -> o f", o=1), kbc, 8, cm)
        emit_cols(b, block, dr['out_b'].rearrange("(o f) -> o f", o=1), obc, 8,
                  cm)
        emit_cols(b, block, dr['memg_b'].rearrange("(o f) -> o f", o=1),
                  mgb.ap().rearrange("p o -> p o") if False else mgb[:], 1, cm,
                  nlast=16)
        emit_cols(b, block, dr['mem_scale'].rearrange("(o f) -> o f", o=1),
                  msc[:], 1, cm, nlast=16)
        emit_cols(b, block, dr['decays'].rearrange("(o f) -> o f", o=1),
                  ldh[:], 1, cm, nlast=16)
        emit_cols(b, block, dr['memv_b'].rearrange("(o f) -> o f", o=1),
                  mvbc, 8, cm)

        if first:
            emit_stream_in_tok(b, block, dr['xi'], xs, cm)
        else:
            emit_stream_in_fm(b, block, dr['xi'], xs)

        @block.vector
        def _(v):
            for c in range(NCH):
                ins = v.memset(xnb[:, c * CWB:c * CWB + 1], 0.0)
            b.inc(ins, 'V')

        emit_ln(b, block, xs, xnb, cm, lnscr, wait=('L', 'V', 'M', 'E'),
                out_cw=8)

        @block.sync
        def _(s):
            b.w(s, 'V', 'P')
            for c in range(NCH):
                b.inc(s.dma_start(out=xscr[c * 128:(c + 1) * 128, :],
                                  in_=xs[:, c * CW + 1:c * CW + 1 + T]), 'S', 16)

        @block.sync
        def _(s):
            b.w(s, 'V', 'P', 'E')
            b.inc(s.dma_start(out=rf.at(0, 512, 16), in_=dr['rio']), 'Lx', 16)

        @block.scalar
        def _(s):
            b.w(s, 'V', 'L')
            s.wait_ge(b.sem('Lx'), b.val('Lx'))
            b.inc(s.activation(ldh[:], ldh[:], AF.Ln), 'E')

        @block.vector
        def _(v):
            b.w(v, 'E')
            b.inc(v.tensor_scalar_mul(ldh2[:], ldh[:], 0.5), 'V')

        @block.scalar
        def _(s):
            b.w(s, 'V')
            b.inc(s.activation(arow[:], rf.at(0, 512, 16), AF.Exp,
                               scale=ldh[:]), 'E')

        def xn(c, i0, ln_, shift=False):
            off = 0 if shift else 1
            return xnb[:, c * CWB + i0 + off: c * CWB + i0 + off + ln_]

        for side, scr_, bcol, scl in ((0, qscr, qbc, 0.125), (1, kscr, kbc, 1.0)):
            for half in range(2):
                @block.gpsimd
                def _(g, side=side, half=half):
                    g.wait_ge(b.sem('M'), b.val('M'))
                    g.wait_ge(b.sem('P'), b.val('P'))
                    for dc in range(NCH):
                        b.inc(g.dma_start(
                            out=wbuf[:, dc * 512:(dc + 1) * 512],
                            in_=dr['qkv_w'][dc * 128:(dc + 1) * 128,
                                            side * 1024 + half * 512:
                                            side * 1024 + half * 512 + 512]),
                            'G', 16)

                for g_ in range(4):
                    for ic in range(4):
                        @block.tensor
                        def _(t, side=side, half=half, g_=g_, ic=ic):
                            b.w(t, 'G', 'V', 'E', 'P')
                            for dc in range(NCH):
                                ins = t.matmul(
                                    pb[ic % 2][:],
                                    wbuf[:, dc * 512 + g_ * 128:
                                         dc * 512 + (g_ + 1) * 128],
                                    xn(dc, ic * 512, 512),
                                    start=(dc == 0), stop=(dc == NCH - 1))
                            b.inc(ins, 'M')

                        @block.scalar
                        def _(s, side=side, half=half, g_=g_, ic=ic, scl=scl,
                              bcol=bcol):
                            s.wait_ge(b.sem('M'), b.val('M'))
                            s.wait_ge(b.sem('S'), b.val('S'))
                            b.inc(s.activation(
                                sc1p[:], pb[ic % 2][:], AF.Identity,
                                bias=bcol[:, half * 4 + g_:half * 4 + g_ + 1],
                                scale=scl), 'E')

                        @block.sync
                        def _(s, side=side, half=half, g_=g_, ic=ic, scr_=scr_):
                            s.wait_ge(b.sem('E'), b.val('E'))
                            r0 = (half * 4 + g_) * 128
                            b.inc(s.dma_start(
                                out=scr_[r0:r0 + 128,
                                         ic * 512:(ic + 1) * 512],
                                in_=sc1p[:]), 'S', 16)

        @block.sync
        def _(s):
            b.w(s, 'V', 'P')
            b.inc(s.dma_start(out=rf.at(4608, 1024, 1),
                              in_=dr['qkv_b'][2048:3072]
                              .rearrange("(o f) -> o f", o=1)), 'Lx', 16)


        @block.vector
        def _(v):
            v.wait_ge(b.sem('Lx'), b.val('Lx'))
            vrow = rf.at(4608, 1024, 1)
            v.tensor_copy(vbh2[:], vrow)
            v.tensor_sub(vrow, vrow, vbh2[:])
            v.tensor_copy(vbl2[:], vrow)
            b.inc(v.engine_nop() if False else v.tensor_copy(
                vbl2[:], vbl2[:]), 'V')

        for half in range(2):
            @block.gpsimd
            def _(g, half=half):
                g.wait_ge(b.sem('M'), b.val('M'))
                for dc in range(NCH):
                    b.inc(g.dma_start(
                        out=wbuf[:, dc * 512:(dc + 1) * 512],
                        in_=dr['qkv_w'][dc * 128:(dc + 1) * 128,
                                        2048 + half * 512:
                                        2048 + half * 512 + 512]), 'G', 16)

            for tt in range(16):
                @block.tensor
                def _(t, half=half, tt=tt):
                    b.w(t, 'G', 'V', 'E', 'P')
                    for dc in range(NCH):
                        t.matmul(pb[4 + tt % 2][:],
                                 xn(dc, tt * 128, 128),
                                 wbuf[:, dc * 512:(dc + 1) * 512],
                                 start=(dc == 0), stop=False)
                    t.matmul(pb[4 + tt % 2][:], cm['ones1b'][:],
                             vbh2[:, half * 512:(half + 1) * 512],
                             start=False, stop=False)
                    ins = t.matmul(pb[4 + tt % 2][:], cm['ones1b'][:],
                                   vbl2[:, half * 512:(half + 1) * 512],
                                   start=False, stop=True)
                    b.inc(ins, 'M')

                @block.scalar
                def _(s, half=half, tt=tt):
                    s.wait_ge(b.sem('M'), b.val('M'))
                    s.wait_ge(b.sem('S'), b.val('S'))
                    b.inc(s.copy(sc1p[:], pb[4 + tt % 2][:]), 'E')

                @block.sync
                def _(s, half=half, tt=tt):
                    s.wait_ge(b.sem('E'), b.val('E'))
                    b.inc(s.dma_start(
                        out=vscr[tt * 128:(tt + 1) * 128,
                                 half * 512:(half + 1) * 512], in_=sc1p[:]),
                        'S', 16)

        for half in range(2):
            @block.gpsimd
            def _(g, half=half):
                g.wait_ge(b.sem('M'), b.val('M'))
                for dc in range(NCH):
                    b.inc(g.dma_start(
                        out=wbuf[:, dc * 512:(dc + 1) * 512],
                        in_=dr['memv_w'][dc * 128:(dc + 1) * 128,
                                         half * 512:(half + 1) * 512]), 'G', 16)

            for g_ in range(4):
                for ic in range(4):
                    @block.tensor
                    def _(t, half=half, g_=g_, ic=ic):
                        b.w(t, 'G', 'V', 'E', 'P')
                        for dc in range(NCH):
                            ins = t.matmul(
                                pb[ic % 2][:],
                                wbuf[:, dc * 512 + g_ * 128:
                                     dc * 512 + (g_ + 1) * 128],
                                xn(dc, ic * 512, 512),
                                start=(dc == 0), stop=(dc == NCH - 1))
                        b.inc(ins, 'M')

                    @block.scalar
                    def _(s, half=half, g_=g_, ic=ic):
                        s.wait_ge(b.sem('M'), b.val('M'))
                        s.wait_ge(b.sem('S'), b.val('S'))
                        b.inc(s.copy(sc1p[:], pb[ic % 2][:]), 'E')

                    @block.sync
                    def _(s, half=half, g_=g_, ic=ic):
                        s.wait_ge(b.sem('E'), b.val('E'))
                        r0 = half * 512 + g_ * 128
                        b.inc(s.dma_start(
                            out=mvscr[r0:r0 + 128, ic * 512:(ic + 1) * 512],
                            in_=sc1p[:]), 'S', 16)

        @block.gpsimd
        def _(g):
            g.wait_ge(b.sem('M'), b.val('M'))
            for dc in range(NCH):
                b.inc(g.dma_start(out=wbuf[:, dc * 16:(dc + 1) * 16],
                                  in_=dr['memg_w'][dc * 128:(dc + 1) * 128, :]),
                      'G', 16)

        for ic in range(4):
            @block.tensor
            def _(t, ic=ic):
                b.w(t, 'G', 'V', 'E')
                for dc in range(NCH):
                    ins = t.matmul(pb[6][0:16, 0:512],
                                   wbuf[:, dc * 16:(dc + 1) * 16],
                                   xn(dc, ic * 512, 512),
                                   start=(dc == 0), stop=(dc == NCH - 1))
                b.inc(ins, 'M')

            @block.scalar
            def _(s, ic=ic):
                s.wait_ge(b.sem('M'), b.val('M'))
                b.inc(s.activation(gate[:, ic * 512:(ic + 1) * 512],
                                   pb[6][0:16, 0:512], AF.Sigmoid,
                                   bias=mgb[:]), 'E')

        def emit_lines(wA_dram, wB_dram, prev_for_A):
            for dstT, wsrc, order, shifted in (
                    (A1, wA_dram, P1ORDER, prev_for_A),
                    (A2, wA_dram, P2ORDER, prev_for_A),
                    (B1, wB_dram, P2ORDER, False),
                    (B2, wB_dram, P1ORDER, False)):
                @block.gpsimd
                def _(g, wsrc=wsrc):
                    g.wait_ge(b.sem('M'), b.val('M'))
                    g.wait_ge(b.sem('V'), b.val('V'))
                    for dc in range(NCH):
                        b.inc(g.dma_start(
                            out=wbuf[:, dc * 64:(dc + 1) * 64],
                            in_=wsrc[dc * 128:(dc + 1) * 128, :]), 'G', 16)

                @block.vector
                def _(v, order=order):
                    v.wait_ge(b.sem('G'), b.val('G'))
                    for dc in range(NCH):
                        for k in range(6):
                            src_ = wbuf[:, dc * 64:(dc + 1) * 64].rearrange(
                                "p (h four) -> p h four", four=4)[:, :, order[k]:order[k] + 1]
                            ins = v.tensor_copy(
                                wbuf[:, 1024 + dc * 96:1024 + (dc + 1) * 96]
                                .rearrange("p (h six) -> p h six", six=6)
                                [:, :, k:k + 1], src_)
                    b.inc(ins, 'V')

                for ic in range(4):
                    @block.tensor
                    def _(t, dstT=dstT, ic=ic):
                        b.w(t, 'V', 'E')
                        for dc in range(NCH):
                            ins = t.matmul(
                                pb[2 + ic % 2][0:96, :],
                                wbuf[:, 1024 + dc * 96:1024 + (dc + 1) * 96],
                                xn(dc, ic * 512, 512, shift=shifted),
                                start=(dc == 0), stop=(dc == NCH - 1))
                        b.inc(ins, 'M')

                    @block.scalar
                    def _(s, dstT=dstT, ic=ic):
                        s.wait_ge(b.sem('M'), b.val('M'))
                        b.inc(s.copy(dstT[:, ic * 512:(ic + 1) * 512],
                                     pb[2 + ic % 2][0:96, :]), 'E')

        def emit_exterior(dst):
            for ic in range(4):
                @block.vector
                def _(v, ic=ic):
                    b.w(v, 'E')
                    v.wait_ge(b.sem('M'), b.val('M'))
                    sl_ = slice(ic * 512, (ic + 1) * 512)
                    v.tensor_tensor(sc1[0:96, :], A1[:, sl_], B1[:, sl_],
                                    ALU.mult)
                    v.tensor_tensor(sc2[0:96, :], A2[:, sl_], B2[:, sl_],
                                    ALU.mult)
                    b.inc(v.tensor_sub(sc1[0:96, :], sc1[0:96, :],
                                       sc2[0:96, :]), 'V')

                @block.scalar
                def _(s, ic=ic):
                    s.wait_ge(b.sem('V'), b.val('V'))
                    b.inc(s.activation(sc2[0:96, :], sc1[0:96, :], AF.Square),
                          'E')

                @block.tensor
                def _(t, ic=ic):
                    t.wait_ge(b.sem('E'), b.val('E'))
                    b.inc(t.matmul(pb[6][0:16, 0:512], sel6[:], sc2[0:96, :],
                                   start=True, stop=True), 'M')

                @block.vector
                def _(v, ic=ic):
                    v.wait_ge(b.sem('M'), b.val('M'))
                    b.inc(v.tensor_scalar(nrows[:], pb[6][0:16, 0:512],
                                          e24b[0:16, 0:1], None, ALU.max), 'V')

                @block.scalar
                def _(s, ic=ic):
                    s.wait_ge(b.sem('V'), b.val('V'))
                    s.activation(nrows[:], nrows[:], AF.Ln)
                    b.inc(s.activation(nrows[:], nrows[:], AF.Exp, scale=-0.5),
                          'E')

                @block.vector
                def _(v, ic=ic):
                    v.wait_ge(b.sem('E'), b.val('E'))
                    v.tensor_copy(sc2[0:16, 0:512], nrows[:])
                    v.tensor_sub(nrows[:], nrows[:], sc2[0:16, 0:512])
                    b.inc(v.tensor_copy(ptb[0:16, 0:512], nrows[:]), 'V')

                @block.tensor
                def _(t, ic=ic):
                    t.wait_ge(b.sem('V'), b.val('V'))
                    t.matmul(pb[7][0:96, :], sel6t[:], sc2[0:16, 0:512],
                             start=True, stop=False)
                    b.inc(t.matmul(pb[7][0:96, :], sel6t[:],
                                   ptb[0:16, 0:512], start=False, stop=True),
                          'M')

                @block.vector
                def _(v, ic=ic, dst=dst):
                    v.wait_ge(b.sem('M'), b.val('M'))
                    b.inc(v.tensor_tensor(dst[:, ic * 512:(ic + 1) * 512],
                                          sc1[0:96, :], pb[7][0:96, :],
                                          ALU.mult), 'V')

        emit_lines(dr['w1_write'], dr['w2_write'], True)
        emit_exterior(wl)
        @block.vector
        def _(v):
            b.w(v, 'V')
            b.inc(v.memset(jrep[:], 0.0), 'V')

        @block.gpsimd
        def _(g):
            g.wait_ge(b.sem('V'), b.val('V'))
            for h in range(16):
                b.inc(g.dma_start(out=jrep[6 * h:6 * h + 6, 6 * h:6 * h + 6],
                                  in_=dr['J']), 'G', 16)

        for ic in range(4):
            @block.tensor
            def _(t, ic=ic):
                b.w(t, 'G', 'V', 'E')
                b.inc(t.matmul(pb[2 + ic % 2][0:96, :], jrep[:],
                               wl[:, ic * 512:(ic + 1) * 512],
                               start=True, stop=True), 'M')

            @block.scalar
            def _(s, ic=ic):
                s.wait_ge(b.sem('M'), b.val('M'))
                b.inc(s.copy(jw[:, ic * 512:(ic + 1) * 512],
                             pb[2 + ic % 2][0:96, :]), 'E')

        emit_lines(dr['w1_read'], dr['w2_read'], False)
        emit_exterior(rl)

        if upto == 'proj':
            emit_stream_in_fm(b, block, xscr, xs)
            if last:
                emit_stream_out_tok(b, block, xs, dr['xo'], cm,
                                    wait=('V', 'P', 'E', 'S'))
            else:
                emit_stream_out_fm(b, block, xs, dr['xo'],
                                   wait=('V', 'P', 'E', 'S'))

            @block.sync
            def _(s):
                for nm in list(b.cnt):
                    b.w(s, nm)
            return nc

        @block.vector
        def _(v):
            b.w(v, 'E', 'S', 'M')
            v.memset(vh[0][:], 1.0)
            b.inc(v.memset(vh[1][:], 1.0), 'V')

        for h in range(16):
            @block.gpsimd
            def _(g, h=h):
                g.wait_ge(b.sem('M'), b.val('M'))
                g.wait_ge(b.sem('V'), b.val('V'))
                if h < 2:
                    g.wait_ge(b.sem('S'), b.val('S'))
                b.inc(g.dma_start(out=kh[h % 2][:],
                                  in_=kscr[h * 64:(h + 1) * 64, :]), 'G', 16)
                b.inc(g.dma_start(out=qh[h % 2][:],
                                  in_=qscr[h * 64:(h + 1) * 64, :]), 'G', 16)
                for jg in range(4):
                    b.inc(g.dma_start(
                        out=vh[h % 2][:].rearrange("p (j c) -> p j c", j=16)
                        [:, jg * 4:(jg + 1) * 4, 0:64],
                        in_=vscr[jg * 512:(jg + 1) * 512, h * 64:(h + 1) * 64]
                        .rearrange("(j p) c -> p j c", p=128)), 'G', 16)

            for ic in range(4):
                jts = list(range(4 * ic + 4))

                def sc_mm(t, jt, h=h, ic=ic):
                    t.wait_ge(b.sem('E'), b.val('E'))
                    t.wait_ge(b.sem('G'), b.val('G'))
                    b.inc(t.matmul(pb[jt % 2][:],
                                   kh[h % 2][:, jt * 128:(jt + 1) * 128],
                                   qh[h % 2][:, ic * 512:(ic + 1) * 512],
                                   start=True, stop=True), 'M')

                def ex_op(s, jt):
                    s.wait_ge(b.sem('M'), b.val('M'))
                    b.inc(s.activation(
                        ptb[:, (jt % 2) * 512:(jt % 2) * 512 + 512],
                        pb[jt % 2][:], AF.Exp), 'E')

                def mask_op(g, jt, ic=ic):
                    d0 = ic * 512 - jt * 128
                    if d0 <= 384:
                        g.wait_ge(b.sem('E'), b.val('E'))
                        sl_ = ptb[:, (jt % 2) * 512:(jt % 2) * 512 + 512]
                        b.inc(g.affine_select(sl_, sl_, pattern=[[1, 512]],
                                              compare_op=ALU.is_ge, fill=0.0,
                                              base=d0, channel_multiplier=-1),
                              'P')

                def pv_mm(t, idx, jt, h=h, ic=ic):
                    t.wait_ge(b.sem('E'), b.val('E'))
                    t.wait_ge(b.sem('P'), b.val('P'))
                    b.inc(t.matmul(pb[2][0:65, :],
                                   vh[h % 2][:].rearrange(
                                       "p (j c) -> p j c", j=16)[:, jt, :],
                                   ptb[:, (jt % 2) * 512:(jt % 2) * 512 + 512],
                                   start=(idx == 0), stop=(idx == len(jts) - 1)),
                          'M')

                for idx, jt in enumerate(jts):
                    if idx >= 1:
                        @block.scalar
                        def _(s, jt=jts[idx - 1]):
                            ex_op(s, jt)

                        @block.gpsimd
                        def _(g, jt=jts[idx - 1]):
                            mask_op(g, jt)

                    @block.tensor
                    def _(t, idx=idx, jt=jt):
                        sc_mm(t, jt)
                        if idx >= 1:
                            pv_mm(t, idx - 1, jts[idx - 1])

                @block.scalar
                def _(s):
                    ex_op(s, jts[-1])

                @block.gpsimd
                def _(g):
                    mask_op(g, jts[-1])

                @block.tensor
                def _(t):
                    pv_mm(t, len(jts) - 1, jts[-1])

                oun = rf.at(0, 512, 65)
                rrow = rf.at(512, 512, 1)

                @block.scalar
                def _(s, h=h, ic=ic):
                    s.wait_ge(b.sem('M'), b.val('M'))
                    s.wait_ge(b.sem('V'), b.val('V'))
                    s.copy(oun[:], pb[2][0:65, :])
                    s.activation(rrow[:], oun[64:65, :], AF.Ln)
                    b.inc(s.activation(rrow[:], rrow[:], AF.Exp, scale=-1.0),
                          'E')

                @block.vector
                def _(v, h=h, ic=ic):
                    b.w(v, 'E')
                    v.wait_ge(b.sem('M'), b.val('M'))
                    b.inc(emit_split_row(b, v, rrow[:], 512, cm), 'V')

                @block.tensor
                def _(t, h=h, ic=ic):
                    b.w(t, 'E', 'V')
                    b.inc(emit_bcast_mm(b, t, pb[3][0:64, :], 512, cm, m=64),
                          'M')

                @block.vector
                def _(v, h=h, ic=ic):
                    v.wait_ge(b.sem('M'), b.val('M'))
                    r0 = (h % 2) * 64
                    b.inc(v.tensor_tensor(
                        oto[r0:r0 + 64,
                            (h // 2) * T + ic * 512:(h // 2) * T + (ic + 1) * 512],
                        oun[0:64, :], pb[3][0:64, :], ALU.mult), 'V')

        for hg in range(4):
            for which, src_, dst_ in ((0, jw, spr1), (1, rl, spr2)):
                for ic in range(4):
                    @block.tensor
                    def _(t, hg=hg, which=which, src_=src_, ic=ic):
                        b.w(t, 'V', 'E', 'P', 'G')
                        b.inc(t.matmul(pb[2 + which][:],
                                       psel[:, hg * 128:(hg + 1) * 128],
                                       src_[:, ic * 512:(ic + 1) * 512],
                                       start=True, stop=True), 'M')

                    @block.scalar
                    def _(s, which=which, dst_=dst_, ic=ic):
                        s.wait_ge(b.sem('M'), b.val('M'))
                        b.inc(s.copy(dst_[:, ic * 512:(ic + 1) * 512],
                                     pb[2 + which][:]), 'E')

            for m in range(4):
                h = 4 * hg + m
                @block.tensor
                def _(t, h=h):
                    b.w(t, 'E', 'V')
                    b.inc(t.transpose(pb[3][0:1, 0:16], ldh2[:],
                                      cm['idf'][0:16, 0:16]), 'M')

                @block.vector
                def _(v, h=h):
                    v.wait_ge(b.sem('M'), b.val('M'))
                    b.inc(v.tensor_copy(ldrow[:], pb[3][0:1, 0:16]), 'V')
                    v.wait_ge(b.sem('V'), b.val('V'))

                @block.vector
                def _(v, h=h):
                    b.inc(v.tensor_copy(cm['hrow'][:, 0:1],
                                        ldrow[:, h:h + 1]), 'V')
                    v.wait_ge(b.sem('V'), b.val('V'))
                    b.inc(v.tensor_sub(cm['l32row'][:, 0:1],
                                       ldrow[:, h:h + 1],
                                       cm['hrow'][:, 0:1]), 'V')
                    v.wait_ge(b.sem('V'), b.val('V'))
                    b.inc(v.tensor_copy(cm['lrow'][:, 0:1],
                                        cm['l32row'][:, 0:1]), 'V')
                    v.wait_ge(b.sem('V'), b.val('V'))

                @block.tensor
                def _(t, h=h):
                    b.w(t, 'V')
                    b.inc(emit_bcast_mm(b, t, pb[3][0:128, 0:1], 1, cm), 'M')

                @block.scalar
                def _(s, h=h):
                    s.wait_ge(b.sem('M'), b.val('M'))
                    b.inc(s.copy(ldcol[:], pb[3][0:128, 0:1]), 'E')

                ld2 = rf.at(2048, 19)

                @block.scalar
                def _(s, h=h):
                    s.wait_ge(b.sem('E'), b.val('E'))
                    b.inc(s.activation(ld2, dio[:], AF.Exp,
                                       scale=ldcol[:]), 'E')

                for ic in range(4):
                    for jt in range(4 * ic + 4):
                        @block.tensor
                        def _(t, hg=hg, m=m, ic=ic, jt=jt):
                            t.wait_ge(b.sem('E'), b.val('E'))
                            t.wait_ge(b.sem('P'), b.val('P'))
                            b.inc(t.matmul(
                                pb[jt % 2][:],
                                spr1[32 * m:32 * m + 6, jt * 128:(jt + 1) * 128],
                                spr2[32 * m:32 * m + 6, ic * 512:(ic + 1) * 512],
                                start=True, stop=True,
                                tile_position=(32 * m, 0)), 'M')

                        @block.scalar
                        def _(s, m=m, ic=ic, jt=jt):
                            s.wait_ge(b.sem('M'), b.val('M'))
                            s.wait_ge(b.sem('P'), b.val('P'))
                            b.inc(s.activation(
                                sc1[:], pb[jt % 2][:], AF.Square,
                                scale=ld2[:, 4 * ic - jt + 3:
                                          4 * ic - jt + 4]),
                                'E')

                        dodiag = (ic * 512 - jt * 128) <= 384

                        if dodiag:
                            @block.gpsimd
                            def _(g, ic=ic, jt=jt):
                                g.wait_ge(b.sem('E'), b.val('E'))
                                b.inc(g.affine_select(
                                    sc1[:], sc1[:], pattern=[[1, 512]],
                                    compare_op=ALU.is_gt, fill=0.0,
                                    base=ic * 512 - jt * 128,
                                    channel_multiplier=-1), 'P')

                        @block.tensor
                        def _(t, hg=hg, m=m, h=h, ic=ic, jt=jt):
                            t.wait_ge(b.sem('E'), b.val('E'))
                            t.wait_ge(b.sem('P'), b.val('P'))
                            first_mm = (hg == 0 and m == 0 and jt == 0)
                            last_mm = (hg == 3 and m == 3 and jt == 4 * ic + 3)
                            b.inc(t.matmul(
                                pb[4 + ic][0:16, 0:512],
                                ehs[:, h * 16:(h + 1) * 16], sc1[:],
                                start=first_mm, stop=last_mm,
                                skip_group_check=True), 'M')

        for ic in range(4):
            @block.scalar
            def _(s, ic=ic):
                s.wait_ge(b.sem('M'), b.val('M'))
                s.wait_ge(b.sem('V'), b.val('V'))
                b.inc(s.copy(nrows[:], pb[4 + ic][0:16, 0:512]), 'E')

            @block.vector
            def _(v, ic=ic):
                v.wait_ge(b.sem('E'), b.val('E'))
                b.inc(v.tensor_tensor(mem[:, ic * 512:(ic + 1) * 512],
                                      nrows[:], arow[:], ALU.mult), 'V')

        @block.scalar
        def _(s):
            s.wait_ge(b.sem('V'), b.val('V'))
            b.inc(s.activation(sg[:], mem[:], AF.Sigmoid, scale=msc[:]), 'E')

        @block.vector
        def _(v):
            v.wait_ge(b.sem('E'), b.val('E'))
            b.inc(v.tensor_tensor(gg[:], sg[:], gate[:], ALU.mult), 'V')

        @block.vector
        def _(v):
            b.inc(v.memset(sc2[0:16, 0:1], 0.0625), 'V')

        for ic in range(4):
            @block.tensor
            def _(t, ic=ic):
                b.w(t, 'V', 'E')
                b.inc(t.matmul(pb[6][0:1, 0:512], sc2[0:16, 0:1],
                               gg[:, ic * 512:(ic + 1) * 512],
                               start=True, stop=True), 'M')

            @block.scalar
            def _(s, ic=ic):
                s.wait_ge(b.sem('M'), b.val('M'))
                b.inc(s.copy(gmrow[:, ic * 512:(ic + 1) * 512],
                             pb[6][0:1, 0:512]), 'E')

        emit_stream_in_fm(b, block, xscr, xs)

        gmbc = ptb[0:128, 0:512]
        for ic in range(4):
            @block.vector
            def _(v, ic=ic):
                b.w(v, 'E')
                v.wait_ge(b.sem('M'), b.val('M'))
                b.inc(emit_split_row(b, v, gmrow[:, ic * 512:(ic + 1) * 512],
                                     512, cm), 'V')

            @block.tensor
            def _(t, ic=ic):
                b.w(t, 'V', 'E')
                b.inc(emit_bcast_mm(b, t, pb[6][:], 512, cm), 'M')

            @block.scalar
            def _(s, ic=ic):
                s.wait_ge(b.sem('M'), b.val('M'))
                s.wait_ge(b.sem('V'), b.val('V'))
                b.inc(s.copy(gmbc, pb[6][:]), 'E')

            for dc in range(NCH):
                @block.gpsimd
                def _(g, dc=dc, ic=ic):
                    g.wait_ge(b.sem('V'), b.val('V'))
                    b.inc(g.dma_start(out=sc1[:],
                                      in_=mvscr[dc * 128:(dc + 1) * 128,
                                                ic * 512:(ic + 1) * 512]),
                          'G', 16)

                @block.vector
                def _(v, dc=dc, ic=ic):
                    b.w(v, 'E')
                    v.wait_ge(b.sem('G'), b.val('G'))
                    v.scalar_tensor_tensor(sc2[:], sc1[:], mvbc[:, dc:dc + 1],
                                           gmbc, ALU.add, ALU.mult)
                    sl_ = oto[:, dc * T + ic * 512: dc * T + (ic + 1) * 512]
                    b.inc(v.tensor_add(sl_, sl_, sc2[:]), 'V')

        for half in range(2):
            @block.gpsimd
            def _(g, half=half):
                g.wait_ge(b.sem('M'), b.val('M'))
                g.wait_ge(b.sem('V'), b.val('V'))
                for dc in range(NCH):
                    b.inc(g.dma_start(
                        out=wbuf[:, dc * 512:(dc + 1) * 512],
                        in_=dr['out_w'][dc * 128:(dc + 1) * 128,
                                        half * 512:(half + 1) * 512]), 'G', 16)

            for g_ in range(4):
                for ic in range(4):
                    @block.tensor
                    def _(t, half=half, g_=g_, ic=ic):
                        b.w(t, 'G', 'V', 'E')
                        for dc in range(NCH):
                            ins = t.matmul(
                                pb[ic % 2][:],
                                wbuf[:, dc * 512 + g_ * 128:
                                     dc * 512 + (g_ + 1) * 128],
                                oto[:, dc * T + ic * 512: dc * T + (ic + 1) * 512],
                                start=(dc == 0), stop=(dc == NCH - 1))
                        b.inc(ins, 'M')

                    @block.vector
                    def _(v, half=half, g_=g_, ic=ic):
                        v.wait_ge(b.sem('M'), b.val('M'))
                        oc = half * 4 + g_
                        xd = xs[:, oc * CW + 1 + ic * 512:
                                oc * CW + 1 + (ic + 1) * 512]
                        b.inc(v.scalar_tensor_tensor(
                            xd, pb[ic % 2][:], obc[:, oc:oc + 1], xd,
                            ALU.add, ALU.add), 'V')

        if last:
            emit_stream_out_tok(b, block, xs, dr['xo'], cm,
                                wait=('V', 'P', 'E', 'S'))
        else:
            emit_stream_out_fm(b, block, xs, dr['xo'], wait=('V', 'P', 'E', 'S'))

        @block.sync
        def _(s):
            for nm in list(b.cnt):
                b.w(s, nm)

    return nc





def _rotary_tables():
    n, DIM_HEAD, WINDOW = T, 64, 512
    half_idx = np.arange(0, DIM_HEAD, 2, dtype=np.float64)
    inv_freq = 1.0 / 10000.0 ** (half_idx / DIM_HEAD)
    t = np.arange(n, dtype=np.float64)
    freqs = t[:, None] * inv_freq[None, :]
    freqs = np.concatenate([freqs, freqs], axis=-1)
    base = (half_idx + 0.4 * DIM_HEAD) / (1.4 * DIM_HEAD)
    power = (t - n // 2) / float(WINDOW // 2)
    scale = base[None, :] ** power[:, None]
    scale = np.concatenate([scale, scale], axis=-1)
    cos = np.cos(freqs).astype(np.float32)
    sin = np.sin(freqs).astype(np.float32)
    scale = scale.astype(np.float32)
    return cos, sin, scale


def _consts_ffn0():
    cos64, sin64, scale64 = _rotary_tables()
    cq = (cos64 * scale64).T
    ck = (cos64 / scale64).T
    sq = (sin64 * scale64).T
    sk = (sin64 / scale64).T
    sgn = np.concatenate([-np.ones((32, 1)), np.ones((32, 1))], 0)
    sgn = sgn.astype(np.float32)
    rotc = np.concatenate([np.tile(cq, (2, 1)), np.tile(ck, (2, 1))], axis=1)
    rots = np.concatenate([np.tile(sq * sgn, (2, 1)), np.tile(sk * sgn, (2, 1))],
                          axis=1)
    sel64 = np.zeros((128, 2), np.float32)
    sel64[0:64, 0] = 1
    sel64[64:128, 1] = 1
    sel64t = np.zeros((2, 128), np.float32)
    sel64t[0, 0:64] = 1
    sel64t[1, 64:128] = 1
    perm32 = np.zeros((128, 128), np.float32)
    for r in range(128):
        perm32[r, r ^ 32] = 1.0
    return {'rotc': rotc.astype(BFNP), 'rots': rots.astype(BFNP),
            'sel64': sel64.astype(BFNP), 'sel64t': sel64t.astype(BFNP),
            'perm32': perm32.astype(BFNP)}


def _consts_attn0():
    sel6 = np.zeros((96, 16), np.float32)
    for h in range(16):
        sel6[6 * h:6 * h + 6, h] = 1
    sel6t = sel6.T.copy()
    psel = np.zeros((96, 512), np.float32)
    for hg in range(4):
        for m in range(4):
            for k in range(6):
                psel[6 * (4 * hg + m) + k, hg * 128 + 32 * m + k] = 1
    ehs = np.zeros((128, 256), np.float32)
    for h in range(16):
        ehs[:, h * 16 + h] = 1
    dio = np.zeros((128, 19), np.float32)
    for m in range(19):
        dio[:, m] = 128 * (m - 3) - np.arange(128)
    rio = np.tile(np.arange(512, dtype=np.float32), (16, 1))
    out = {'sel6': sel6, 'sel6t': sel6t, 'psel': psel, 'ehs': ehs}
    out = {k: v.astype(BFNP) for k, v in out.items()}
    out['dio'] = dio
    out['rio'] = rio
    return out




def kernel(**inputs):
    F = np.float32
    x = np.asarray(inputs['L0_attn_x'], F)
    B = x.shape[0]
    cores = list(range(B))
    idf = np.eye(128, dtype=F)
    p0f = {k[7:]: np.asarray(v, F) for k, v in inputs.items()
           if k.startswith('L0_ffn_')}
    p0a = {k[8:]: np.asarray(v, F) for k, v in inputs.items()
           if k.startswith('L0_attn_') and k != 'L0_attn_x'}
    p1f = {k[7:]: np.asarray(v, F) for k, v in inputs.items()
           if k.startswith('L1_ffn_')}
    p1a = {k[8:]: np.asarray(v, F) for k, v in inputs.items()
           if k.startswith('L1_attn_')}

    cf0 = _consts_ffn0()
    ca0 = _consts_attn0()

    im_f0 = {'idf': idf}
    im_f0.update(cf0)
    for nm in ['wqkv', 'wo', 'w1', 'w2', 'q_scale', 'k_scale', 'ln_g',
               'ln_b', 'b1', 'b2']:
        im_f0[nm] = p0f[nm]
    im_a0 = {'idf': idf}
    im_a0.update(ca0)
    for nm in ['qkv_w', 'qkv_b', 'w1_write', 'w2_write', 'w1_read', 'w2_read',
               'memv_w', 'memv_b', 'memg_w', 'memg_b', 'mem_scale', 'out_w',
               'out_b', 'J', 'decays']:
        im_a0[nm] = p0a[nm]
    im_f1 = {'idf': idf}
    for nm in ['q_w', 'q_b', 'k_w', 'k_b', 'v_w', 'v_b', 'o_w', 'o_b',
               'router_w', 'router_b', 'w1', 'b1', 'w2', 'b2']:
        im_f1[nm] = p1f[nm]
    im_a1 = {'idf': idf, 'keys': p1a['keys'][0], 'values': p1a['values'][0]}
    for nm in ['Wq_w', 'Wq_b', 'Wk_w', 'Wk_b', 'Wv_w', 'Wv_b']:
        im_a1[nm] = p1a[nm]

    launches = [
        (build_ffn0(first=True), im_f0),
        (build_attn0(), im_a0),
        (build_ffn0(), im_f0),
        (build_ffn1(), im_f1),
        (build_attn1(), im_a1),
        (build_ffn1(last=True), im_f1),
    ]
    cur = [np.ascontiguousarray(x[bb]) for bb in cores]
    for nc, im in launches:
        maps = [dict(im, xi=cur[bb]) for bb in cores]
        res = run_bass_kernel_spmd(nc, maps, core_ids=cores)
        cur = [np.asarray(res.results[bb]['xo']) for bb in cores]
    return np.stack(cur, 0).astype(np.float32)
